# version 04
# baseline (speedup 1.0000x reference)
import sys
sys.path.insert(0, "/opt/trn_rl_repo")
from contextlib import ExitStack
import numpy as np
import ml_dtypes
import concourse.bass as bass
import concourse.mybir as mybir
from concourse.bass_utils import run_bass_kernel_spmd

F32 = mybir.dt.float32
BF16 = mybir.dt.bfloat16
I32 = mybir.dt.int32
AF = mybir.ActivationFunctionType
ALU = mybir.AluOpType
AX = mybir.AxisListType
BF = ml_dtypes.bfloat16

B, S, E = 4, 2048, 2048
RSQ = 0.08838834764831845
LN2 = 0.6931471805599453
INVLN2 = 1.4426950408889634
NSLOT_TILES = 4 * (8 + 16)


def own_tiles(c):
    return [(s, w) for s in range(B) for w in (c, 15 - c)]


def slot_offsets():
    out, off = [], 0
    for _s in range(B):
        for L in (8, 16):
            out.append((off, L))
            off += L
    return out


class SemC:
    def __init__(self, h):
        self.h = h
        self.n = 0


class Em:
    ENGS = ("sync", "gpsimd", "vector", "scalar", "tensor")

    SELF_SYNC = ("vector", "scalar")

    def __init__(self, nc, es):
        self.nc = nc
        self.es = es
        self.st = {k: [] for k in self.ENGS}
        self._sems = {}
        self._last = {}

    def sem(self, name):
        if isinstance(name, SemC):
            return name
        if name not in self._sems:
            h = self.es.enter_context(self.nc.semaphore(name))
            self._sems[name] = SemC(h)
        return self._sems[name]

    def op(self, eng, build, inc=None, k=1):
        if eng in self.SELF_SYNC:
            prev = self._last.get(eng)
            if prev is not None:
                self.st[eng].append(lambda e, h=prev[0], vv=prev[1]: e.wait_ge(h, vv))
            if inc is None:
                inc = eng + "_selfsync"
        if inc is not None:
            s = self.sem(inc)
            s.n += k
            self.st[eng].append(lambda e, b=build, h=s.h, kk=k: b(e).then_inc(h, kk))
            if eng in self.SELF_SYNC:
                self._last[eng] = (s.h, s.n)
        else:
            self.st[eng].append(lambda e, b=build: b(e))

    def dma(self, eng, out, in_, inc, **kw):
        self.op(eng, lambda e, o=out, i=in_, kw2=dict(kw): e.dma_start(out=o, in_=i, **kw2),
                inc, 16)

    def wait(self, eng, sem, val=None):
        s = self.sem(sem)
        v = s.n if val is None else val
        if v > 0:
            self.st[eng].append(lambda e, h=s.h, vv=v: e.wait_ge(h, vv))

    def finalize(self):
        with self.nc.Block() as block:
            @block.sync
            def _(e):
                for f in self.st["sync"]:
                    f(e)

            @block.gpsimd
            def _(e):
                for f in self.st["gpsimd"]:
                    f(e)

            @block.vector
            def _(e):
                for f in self.st["vector"]:
                    f(e)

            @block.scalar
            def _(e):
                for f in self.st["scalar"]:
                    f(e)

            @block.tensor
            def _(e):
                for f in self.st["tensor"]:
                    f(e)



def emit_round(em, eng, v_ap, r_out, i_scr, f1, f2, inc=None):
    em.op(eng, lambda e: e.tensor_scalar(i_scr, v_ap, 0.5, None, ALU.add))
    em.op(eng, lambda e: e.tensor_tensor(f1, v_ap, i_scr, ALU.subtract))
    em.op(eng, lambda e: e.tensor_single_scalar(f2, f1, 0.5, ALU.is_ge))
    em.op(eng, lambda e: e.tensor_single_scalar(f1, f1, -0.5, ALU.is_lt))
    em.op(eng, lambda e: e.tensor_tensor(f2, f2, f1, ALU.subtract))
    em.op(eng, lambda e: e.tensor_tensor(r_out, f2, i_scr, ALU.add), inc=inc)


def emit_e4m3(em, bamax_ap, inv6ts_col, invts_col, bsq, invd, scr, sv, ss, bm3, bp3):
    V, SC = "vector", "scalar"
    f1, f2, f3, i1 = scr["f1"][:, :], scr["f2"][:, :], scr["f3"][:, :], scr["i1"][:, :]
    bq = bsq[:, :]
    iv = invd[:, :]
    em.op(V, lambda e: e.tensor_scalar(f1, bamax_ap, inv6ts_col, None, ALU.mult))
    em.op(V, lambda e: e.tensor_scalar(f1, f1, 448.0, 2.44140625e-4, ALU.min, ALU.max), inc=sv)
    em.wait(SC, sv)
    em.op(SC, lambda e: e.activation(f2, f1, AF.Ln), inc=ss)
    em.wait(V, ss)
    em.op(V, lambda e: e.tensor_single_scalar(f2, f2, INVLN2, ALU.mult))
    em.op(V, lambda e: e.tensor_scalar(i1, f2, 0.0, None, ALU.add))
    em.op(V, lambda e: e.tensor_tensor(f3, f2, i1, ALU.subtract))
    em.op(V, lambda e: e.tensor_single_scalar(f3, f3, 0.0, ALU.is_lt))
    em.op(V, lambda e: e.tensor_tensor(f2, i1, f3, ALU.subtract))
    em.op(V, lambda e: e.tensor_scalar(f2, f2, -6.0, 8.0, ALU.max, ALU.min), inc=sv)
    em.wait(SC, sv)
    em.op(SC, lambda e: e.activation(f3, f2, AF.Exp, bias=bm3, scale=LN2))
    em.op(SC, lambda e: e.activation(f2, f2, AF.Exp, bias=bp3, scale=-LN2), inc=ss)
    em.wait(V, ss)
    em.op(V, lambda e: e.tensor_tensor(f1, f1, f2, ALU.mult))
    emit_round(em, V, f1, f1, i1, f2, bq)
    em.op(V, lambda e: e.tensor_tensor(bq, f1, f3, ALU.mult))
    em.op(V, lambda e: e.tensor_single_scalar(f2, bq, 0.0, ALU.is_equal))
    em.op(V, lambda e: e.tensor_tensor(f2, bq, f2, ALU.add))
    em.op(V, lambda e: e.reciprocal(f2, f2))
    em.op(V, lambda e: e.tensor_scalar(iv, f2, invts_col, None, ALU.mult))


def emit_e2m1(em, xin_ap, invd, bsq, xdq_ap, scr, sv, ss, inc=None):
    V, SC = "vector", "scalar"
    af, tf = scr["af"][:, :], scr["tf"][:, :]
    a2, c2, c4, f2 = (scr[k][:, :] for k in ("a2", "c2", "c4", "f2"))
    i1 = scr["i1"][:, :]

    def b3(t):
        return t[:, 0:128].rearrange("p (b one) -> p b one", one=1).broadcast_to([128, 128, 16])

    def v3(t):
        return t.rearrange("p (b k) -> p b k", k=16)

    em.op(V, lambda e: e.tensor_tensor(v3(af), v3(xin_ap), b3(invd), ALU.mult))
    em.op(V, lambda e: e.tensor_scalar(af, af, 6.0, -6.0, ALU.min, ALU.max), inc=sv)
    em.wait(SC, sv)
    em.op(SC, lambda e: e.activation(a2, af, AF.Square), inc=ss)
    em.wait(V, ss)
    em.op(V, lambda e: e.tensor_single_scalar(c2, a2, 4.0, ALU.is_lt))
    em.op(V, lambda e: e.tensor_single_scalar(c4, a2, 16.0, ALU.is_lt))
    em.op(V, lambda e: e.tensor_scalar(a2, c4, 0.5, 0.5, ALU.mult, ALU.add))
    em.op(V, lambda e: e.tensor_tensor(a2, a2, c2, ALU.add))
    em.op(V, lambda e: e.tensor_tensor(tf, af, a2, ALU.mult))
    em.op(V, lambda e: e.tensor_scalar(i1, tf, 0.5, None, ALU.add))
    em.op(V, lambda e: e.tensor_tensor(af, tf, i1, ALU.subtract))
    em.op(V, lambda e: e.tensor_single_scalar(f2, af, 0.5, ALU.is_ge))
    em.op(V, lambda e: e.tensor_single_scalar(af, af, -0.5, ALU.is_lt))
    em.op(V, lambda e: e.tensor_tensor(f2, f2, af, ALU.subtract))
    em.op(V, lambda e: e.tensor_tensor(tf, f2, i1, ALU.add))
    em.op(V, lambda e: e.tensor_scalar(c4, c4, -1.0, 2.0, ALU.mult, ALU.add))
    em.op(V, lambda e: e.scalar_tensor_tensor(c4, c2, -0.5, c4, ALU.mult, ALU.add))
    em.op(V, lambda e: e.tensor_tensor(tf, tf, c4, ALU.mult))
    em.op(V, lambda e: e.tensor_tensor(v3(xdq_ap), v3(tf), b3(bsq), ALU.mult), inc=inc)


def emit_int8_cols(em, amax_ap, inv_ap, scale_ap, s0, s1):
    V = "vector"
    em.op(V, lambda e: e.tensor_single_scalar(s0, amax_ap, 0.0, ALU.is_equal))
    em.op(V, lambda e: e.tensor_tensor(s1, amax_ap, s0, ALU.add))
    em.op(V, lambda e: e.reciprocal(s1, s1))
    em.op(V, lambda e: e.tensor_single_scalar(s1, s1, 127.0, ALU.mult))
    em.op(V, lambda e: e.scalar_tensor_tensor(inv_ap, s0, -126.0, s1, ALU.mult, ALU.add))
    em.op(V, lambda e: e.tensor_single_scalar(s1, amax_ap, 1.0 / 127.0, ALU.mult))
    em.op(V, lambda e: e.tensor_tensor(scale_ap, s1, s0, ALU.add))



def build_prep(dbg=False):
    nc = bass.Bass("TRN2", target_bir_lowering=False)
    D = {}
    def din(n, sh, dt=F32):
        D[n] = nc.dram_tensor(n, list(sh), dt, kind="ExternalInput").ap()
    def dout(n, sh, dt=F32):
        D[n] = nc.dram_tensor(n, list(sh), dt, kind="ExternalOutput").ap()

    din("w0_full", (E, E))
    din("w0_own", (256, E))
    din("k1_own", (E, 256))
    din("wq1_own", (256, E))
    din("wk1_own", (256, E))
    din("wv1_own", (256, E))
    din("wo1_own", (256, E))
    din("ident", (128, 128), BF16)
    din("ident_f", (128, 128))
    dout("wts", (1, 1))
    dout("wdqT_s", (E, 256), BF16)
    dout("kq_s", (E, 256), BF16)
    dout("kscale_s", (256, 1))
    dout("wqT_s", (E, 256), BF16)
    dout("wkT_s", (E, 256), BF16)
    dout("wvT_s", (E, 256), BF16)
    dout("woT_s", (E, 256), BF16)

    with ExitStack() as es:
        em = Em(nc, es)
        sb = lambda n, sh, dt=F32: es.enter_context(nc.sbuf_tensor(n, sh, dt))
        ps = lambda n, sh, dt=F32: es.enter_context(nc.psum_tensor(n, sh, dt))

        wf = [sb(f"wf{i}", [128, E]) for i in range(2)]
        wbf = sb("wbf", [128, E], BF16)
        bamax = sb("bamax", [128, E])
        colmax = sb("colmax", [128, 1])
        rowmax = sb("rowmax", [1, 128])
        wts_sb = sb("wts_sb", [1, 4])
        inv6ts_col = sb("inv6ts_col", [128, 1])
        invts_col = sb("invts_col", [128, 1])
        ones_f = sb("ones_f", [1, 512])
        id_sb = sb("id_sb", [128, 128], BF16)
        id_f = sb("id_f", [128, 128])
        e4scr = {"f1": sb("e4_f1", [128, 128]), "f2": sb("e4_f2", [128, 128]),
                 "f3": sb("e4_f3", [128, 128]), "i1": sb("e4_i1", [128, 128], I32)}
        pf_a = sb("pf_a", [128, E])
        pf_t = sb("pf_t", [128, E])
        qscr = {k: sb("q_" + k, [128, E], BF16) for k in ("a2", "c2", "c4", "f2")}
        qscr["af"] = pf_a
        qscr["tf"] = pf_t
        qscr["i1"] = sb("q_i1", [128, E], I32)
        bsq = sb("bsq", [128, 128])
        invd = sb("invd", [128, 128])
        wdq = sb("wdq", [128, E], BF16)
        stage = sb("stage", [128, 16 * 256], BF16)
        kT = [sb(f"kT{i}", [128, E]) for i in range(2)]
        kqb = sb("kqb", [128, E], BF16)
        kamax = sb("kamax", [128, 2])
        kinv = sb("kinv", [128, 2])
        kscale = sb("kscale", [128, 2])
        small = sb("small", [128, 8])
        cst = sb("cst", [128, 2])
        ps_t = [ps(f"ps_t{i}", [128, 1024], BF16) for i in range(2)]
        ps_tf = [ps(f"ps_tf{i}", [128, 512]) for i in range(2)]
        ps_c = ps("ps_c", [128, 512])

        gld = em.sem("gld")
        ld_id = em.sem("ld_id")
        ld_w0 = em.sem("ld_w0")
        ld_w1 = em.sem("ld_w1")
        v = em.sem("v")
        sc = em.sem("sc")
        pe = em.sem("pe")
        st = em.sem("st")
        sv = em.sem("sv")
        ss = em.sem("ss")

        em.dma("sync", id_sb[:, :], D["ident"][:, :], ld_id)
        em.dma("sync", id_f[:, :], D["ident_f"][:, :], ld_id)
        em.op("vector", lambda e: e.memset(ones_f[:, :], 1.0))
        em.op("vector", lambda e: e.memset(cst[:, 0:1], -3.0 * LN2))
        em.op("vector", lambda e: e.memset(cst[:, 1:2], 3.0 * LN2), inc=v)
        em.wait("scalar", v)

        vmarks = []
        for ch in range(16):
            par = ch % 2
            if ch >= 2:
                em.wait("sync", v, vmarks[ch - 2])
            em.dma("sync", wf[par][:, :], D["w0_full"][128 * ch:128 * (ch + 1), :],
                   ld_w0 if par == 0 else ld_w1)
            em.wait("vector", ld_w0 if par == 0 else ld_w1)
            em.op("vector", lambda e, c=ch, p=par: e.tensor_reduce(
                bamax[:, 128 * c:128 * (c + 1)].rearrange("p (b k) -> p b k", k=16),
                wf[p][:, :].rearrange("p (b k) -> p b k", k=16),
                AX.X, ALU.max, apply_absolute_value=True), inc=v)
            vmarks.append(v.n)
        em.op("vector", lambda e: e.tensor_reduce(colmax[:, 0:1], bamax[:, :], AX.X,
                                                  ALU.max, apply_absolute_value=True), inc=v)
        em.wait("tensor", v)
        em.wait("tensor", ld_id)
        em.op("tensor", lambda e: e.transpose(ps_tf[0][0:1, 0:128], colmax[:, 0:1], id_f[:, :]), inc=pe)
        em.wait("vector", pe)
        em.op("vector", lambda e: e.tensor_copy(rowmax[0:1, :], ps_tf[0][0:1, 0:128]))
        em.op("vector", lambda e: e.tensor_reduce(wts_sb[0:1, 0:1], rowmax[0:1, :], AX.X,
                                                  ALU.max, apply_absolute_value=True))
        em.op("vector", lambda e: e.tensor_single_scalar(wts_sb[0:1, 0:1], wts_sb[0:1, 0:1],
                                                         1.0 / 2688.0, ALU.mult))
        em.op("vector", lambda e: e.tensor_single_scalar(wts_sb[0:1, 1:2], wts_sb[0:1, 0:1],
                                                         6.0, ALU.mult))
        em.op("vector", lambda e: e.reciprocal(wts_sb[0:1, 1:2], wts_sb[0:1, 1:2]))
        em.op("vector", lambda e: e.reciprocal(wts_sb[0:1, 2:3], wts_sb[0:1, 0:1]), inc=v)
        em.wait("sync", v)
        em.dma("sync", D["wts"][:, :], wts_sb[0:1, 0:1], st)
        em.wait("tensor", v)
        em.op("tensor", lambda e: e.matmul(ps_c[:, 0:1], ones_f[0:1, 0:128], wts_sb[0:1, 1:2],
                                           start=True, stop=True))
        em.op("tensor", lambda e: e.matmul(ps_c[:, 1:2], ones_f[0:1, 0:128], wts_sb[0:1, 2:3],
                                           start=True, stop=True), inc=pe)
        em.wait("vector", pe)
        em.op("vector", lambda e: e.tensor_copy(inv6ts_col[:, 0:1], ps_c[:, 0:1]))
        em.op("vector", lambda e: e.tensor_copy(invts_col[:, 0:1], ps_c[:, 1:2]), inc=v)

        for cc in range(2):
            em.wait("sync", v, v.n)
            em.dma("sync", wf[0][:, :], D["w0_own"][128 * cc:128 * (cc + 1), :], ld_w0)
            em.wait("vector", ld_w0)
            em.op("vector", lambda e: e.tensor_reduce(
                bamax[:, 0:128].rearrange("p (b k) -> p b k", k=16),
                wf[0][:, :].rearrange("p (b k) -> p b k", k=16),
                AX.X, ALU.max, apply_absolute_value=True))
            emit_e4m3(em, bamax[:, 0:128], inv6ts_col[:, 0:1], invts_col[:, 0:1],
                      bsq, invd, e4scr, sv, ss, cst[:, 0:1], cst[:, 1:2])
            emit_e2m1(em, wf[0][:, :], invd, bsq, wdq[:, :], qscr, sv, ss, inc=v)
            em.wait("tensor", v)
            for q4 in range(4):
                pst = ps_t[q4 % 2]
                em.wait("tensor", sc, sc.n - 1)
                for j4 in range(4):
                    j = 4 * q4 + j4
                    em.op("tensor", lambda e, jj=j, p=pst, o=j4: e.transpose(
                        p[:, 128 * o:128 * (o + 1)], wdq[:, 128 * jj:128 * (jj + 1)],
                        id_sb[:, :]), inc=(pe if j4 == 3 else None))
                em.wait("scalar", pe)
                em.op("scalar", lambda e, q=q4, c=cc, p=pst: e.copy(
                    stage[:, :].rearrange("p (kj n) -> p kj n", n=256)[:, 4 * q:4 * (q + 1), 128 * c:128 * (c + 1)],
                    p[:, 0:512].rearrange("p (j o) -> p j o", o=128)), inc=sc)
        em.wait("sync", sc)
        for kj in range(16):
            em.dma("sync", D["wdqT_s"][128 * kj:128 * (kj + 1), :],
                   stage[:, 256 * kj:256 * (kj + 1)], st)

        em.wait("sync", v, v.n)
        em.wait("sync", pe, pe.n)
        for dc in range(16):
            em.wait("sync", pe, pe.n)
            em.dma("sync", wf[1][:, 0:256], D["k1_own"][128 * dc:128 * (dc + 1), :], ld_w1)
            em.wait("tensor", ld_w1)
            em.wait("tensor", sc, sc.n)
            for fb in range(2):
                em.op("tensor", lambda e, f=fb: e.transpose(
                    ps_tf[f][:, 0:128], wf[1][:, 128 * f:128 * (f + 1)], id_f[:, :]),
                    inc=(pe if fb == 1 else None))
            em.wait("scalar", pe)
            for fb in range(2):
                em.op("scalar", lambda e, f=fb, d=dc: e.copy(
                    kT[f][:, 128 * d:128 * (d + 1)], ps_tf[f][:, 0:128]), inc=sc)
        em.wait("vector", sc)
        for fb in range(2):
            em.op("vector", lambda e, f=fb: e.tensor_reduce(
                kamax[:, f:f + 1], kT[f][:, :], AX.X, ALU.max, apply_absolute_value=True))
            emit_int8_cols(em, kamax[:, fb:fb + 1], kinv[:, fb:fb + 1], kscale[:, fb:fb + 1],
                           small[:, 0:1], small[:, 1:2])
            em.op("vector", lambda e, f=fb: e.tensor_scalar(
                pf_a[:, :], kT[f][:, :], kinv[:, f:f + 1], None, ALU.mult))
            emit_round(em, "vector", pf_a[:, :], pf_t[:, :], qscr["i1"][:, :],
                       wf[0][:, :], qscr["f2"][:, :])
            em.op("vector", lambda e: e.tensor_scalar(pf_t[:, :], pf_t[:, :],
                                                      127.0, -127.0, ALU.min, ALU.max))
            em.op("vector", lambda e, f=fb: e.tensor_copy(kT[f][:, :], pf_t[:, :]), inc=v)
        em.wait("sync", v)
        em.dma("sync", D["kscale_s"][0:128, :], kscale[:, 0:1], st)
        em.dma("sync", D["kscale_s"][128:256, :], kscale[:, 1:2], st)
        em.wait("scalar", v)
        em.wait("scalar", st, st.n)
        em.wait("tensor", v)
        for dj in range(16):
            em.wait("scalar", pe, pe.n)
            for fb in range(2):
                em.op("scalar", lambda e, d=dj, f=fb: e.copy(
                    kqb[:, 128 * f:128 * (f + 1)], kT[f][:, 128 * d:128 * (d + 1)]), inc=sc)
            em.wait("tensor", sc)
            for fb in range(2):
                em.op("tensor", lambda e, f=fb: e.transpose(
                    ps_t[f][:, 0:128], kqb[:, 128 * f:128 * (f + 1)], id_sb[:, :]),
                    inc=(pe if fb == 1 else None))
            em.wait("scalar", pe)
            for fb in range(2):
                em.op("scalar", lambda e, d=dj, f=fb: e.copy(
                    stage[:, 256 * d + 128 * f:256 * d + 128 * (f + 1)], ps_t[f][:, 0:128]), inc=sc)
        em.wait("sync", sc)
        for dj in range(16):
            em.dma("sync", D["kq_s"][128 * dj:128 * (dj + 1), :],
                   stage[:, 256 * dj:256 * (dj + 1)], st)

        for wname, oname in (("wq1_own", "wqT_s"), ("wk1_own", "wkT_s"),
                             ("wv1_own", "wvT_s"), ("wo1_own", "woT_s")):
            em.wait("scalar", st, st.n)
            for oc in range(2):
                em.wait("gpsimd", pe, pe.n)
                em.dma("gpsimd", wbf[:, :], D[wname][128 * oc:128 * (oc + 1), :], gld)
                em.wait("tensor", gld)
                for cj in range(16):
                    par = cj % 2
                    em.wait("tensor", sc, sc.n - 1)
                    em.op("tensor", lambda e, c=cj, p=par: e.transpose(
                        ps_t[p][:, 0:128], wbf[:, 128 * c:128 * (c + 1)], id_sb[:, :]), inc=pe)
                    em.wait("scalar", pe)
                    em.op("scalar", lambda e, c=cj, o=oc, p=par: e.copy(
                        stage[:, 256 * c + 128 * o:256 * c + 128 * (o + 1)], ps_t[p][:, 0:128]), inc=sc)
            em.wait("sync", sc)
            for cj in range(16):
                em.dma("sync", D[oname][128 * cj:128 * (cj + 1), :],
                       stage[:, 256 * cj:256 * (cj + 1)], st)

        if dbg:
            for nm, sh in (("d_bamax", (128, E)), ("d_colmax", (128, 1)),
                           ("d_rowmax", (1, 128)), ("d_wts4", (1, 4)),
                           ("d_kamax", (128, 2)), ("d_i6", (128, 1)), ("d_it", (128, 1)),
                           ("d_bsq", (128, 128)), ("d_invd", (128, 128))):
                dout(nm, sh)
            em.wait("sync", v)
            em.wait("sync", sc)
            em.wait("sync", pe)
            em.dma("sync", D["d_bamax"][:, :], bamax[:, :], st)
            em.dma("sync", D["d_colmax"][:, :], colmax[:, :], st)
            em.dma("sync", D["d_rowmax"][:, :], rowmax[0:1, :], st)
            em.dma("sync", D["d_wts4"][:, :], wts_sb[0:1, :], st)
            em.dma("sync", D["d_kamax"][:, :], kamax[:, :], st)
            em.dma("sync", D["d_i6"][:, :], inv6ts_col[:, :], st)
            em.dma("sync", D["d_it"][:, :], invts_col[:, :], st)
            em.dma("sync", D["d_bsq"][:, :], bsq[:, :], st)
            em.dma("sync", D["d_invd"][:, :], invd[:, :], st)
        em.wait("gpsimd", st)
        em.finalize()
    return nc



def build_A(layer):
    nc = bass.Bass("TRN2", target_bir_lowering=False)
    D = {}
    def din(n, sh, dt=F32):
        D[n] = nc.dram_tensor(n, list(sh), dt, kind="ExternalInput").ap()
    def dout(n, sh, dt=F32):
        D[n] = nc.dram_tensor(n, list(sh), dt, kind="ExternalOutput").ap()

    kvw = 512 if layer == 0 else 2048
    din("x_own", (1024, E))
    din("ident", (128, 128), BF16)
    if layer == 0:
        din("wq", (16, E, 128))
        din("wk", (4, E, 128))
        din("wv", (4, E, 128))
        din("bq_row", (1, E))
        din("bk_row", (1, 512))
        din("bv_row", (1, 512))
        din("wdqT", (E, E), BF16)
        din("ffn_bias", (1, E))
        din("wts", (1, 1))
        din("xts", (1, 1))
    else:
        din("wqT", (E, E), BF16)
        din("wkT", (E, E), BF16)
        din("wvT", (E, E), BF16)
        din("kq", (E, E), BF16)
        din("kscale_row", (1, E))
        din("ffn_bias", (1, E))
    dout("qfm", (E, 1024), BF16)
    dout("kfm", (kvw, 1024), BF16)
    dout("vtm", (1024, kvw), BF16)
    dout("g", (1024, E))

    with ExitStack() as es:
        em = Em(nc, es)
        sb = lambda n, sh, dt=F32: es.enter_context(nc.sbuf_tensor(n, sh, dt))
        ps = lambda n, sh, dt=F32: es.enter_context(nc.psum_tensor(n, sh, dt))

        gsb = sb("gsb", [128, E])
        xnf = sb("xnf", [128, E])
        xq = sb("xq", [128, E], BF16)
        xqT = sb("xqT", [128, 16 * 128], BF16)
        stage = [sb(f"stage{i}", [128, 1024], BF16) for i in range(2)]
        cols = sb("cols", [128, 8])
        cols2 = sb("cols2", [128, 4])
        stats = sb("stats", [128, 32])
        srow = sb("srow", [1, 8])
        ones_bf = sb("ones_bf", [1, 512], BF16)
        ones_f = sb("ones_f", [1, 512])
        id_sb = sb("id_sb", [128, 128], BF16)
        ffnb_bf = sb("ffnb_bf", [1, E], BF16)
        cst = sb("cst", [128, 3])
        rows = sb("rows", [1, 5 * E])
        rowi = sb("rowi", [1, E], I32)
        i1t = sb("i1t", [128, E], I32)
        f2b = sb("f2b", [128, E], BF16)
        if layer == 0:
            QA = sb("QA", [128, E])
            QT = sb("QT", [128, E])
            qscr = {"af": QA, "tf": QT, "i1": i1t, "f2": f2b,
                    "a2": sb("q_a2", [128, E], BF16),
                    "c2": sb("q_c2", [128, E], BF16),
                    "c4": sb("q_c4", [128, E], BF16)}
            bamax = sb("bamax", [128, 128])
            bsq = sb("bsq", [128, 128])
            invd = sb("invd", [128, 128])
            e4scr = {"f1": sb("e4_f1", [128, 128]), "f2": sb("e4_f2", [128, 128]),
                     "f3": sb("e4_f3", [128, 128]), "i1": sb("e4_i1", [128, 128], I32)}
            xtbuf = [gsb, QA]
        else:
            ysb = sb("ysb", [128, E])
            skb = sb("skb", [128, E])
            bqb = sb("bqb", [128, E])
            xtbuf = [gsb, ysb]

        ps_mm = [ps(f"ps_mm{i}", [128, 512]) for i in range(2)]
        ps_tr = [ps(f"ps_tr{i}", [128, 1024], BF16) for i in range(2)]
        ps_bc = ps("ps_bc", [128, 512])

        ld_id = em.sem("ld_id")
        ld_rows = em.sem("ld_rows")
        ldx = [em.sem("ldx0"), em.sem("ldx1")]
        ld_r = [em.sem("ld_r0"), em.sem("ld_r1")]
        ld_wf = em.sem("ld_wf")
        ld_xnf = em.sem("ld_xnf")
        gld = em.sem("gld")
        v = em.sem("v")
        scs = em.sem("scs")
        pe = em.sem("pe")
        evq = em.sem("evq")
        evt = em.sem("evt")
        st_s = [em.sem("st_s0"), em.sem("st_s1")]
        st_g = em.sem("st_g")
        sv = em.sem("sv")
        ss = em.sem("ss")

        em.dma("sync", id_sb[:, :], D["ident"][:, :], ld_id)
        em.op("vector", lambda e: e.memset(ones_bf[:, :], 1.0))
        em.op("vector", lambda e: e.memset(ones_f[:, :], 1.0))
        em.op("vector", lambda e: e.memset(cst[:, 0:1], -3.0 * LN2))
        em.op("vector", lambda e: e.memset(cst[:, 1:2], 3.0 * LN2))
        em.op("vector", lambda e: e.memset(cst[:, 2:3], 1e-5), inc=v)
        em.wait("scalar", v)

        if layer == 0:
            em.dma("sync", rows[0:1, 0:E], D["bq_row"][:, :], ld_rows)
            em.dma("sync", rows[0:1, E:E + 512], D["bk_row"][:, :], ld_rows)
            em.dma("sync", rows[0:1, E + 512:E + 1024], D["bv_row"][:, :], ld_rows)
            em.dma("sync", rows[0:1, 2 * E:3 * E], D["ffn_bias"][:, :], ld_rows)
            em.dma("sync", srow[0:1, 0:1], D["wts"][:, :], ld_rows)
            em.dma("sync", srow[0:1, 1:2], D["xts"][:, :], ld_rows)
        else:
            em.dma("sync", rows[0:1, 0:E], D["kscale_row"][:, :], ld_rows)
            em.dma("sync", rows[0:1, E:2 * E], D["ffn_bias"][:, :], ld_rows)
        rows_ld = ld_rows.n

        es2 = ExitStack()
        xnT = es2.enter_context(nc.sbuf_tensor("xnT", [128, 16 * 1024], BF16))
        rows_bf = es2.enter_context(nc.sbuf_tensor("rows_bf", [1, 2 * E], BF16))
        wR = [es2.enter_context(nc.sbuf_tensor(f"wR{i}", [128, 16 * 512], BF16)) for i in range(2)]

        xt_free = [(0, 0), (0, 0)]
        for i in range(8):
            par = i % 2
            em.wait("sync", v, xt_free[par][0])
            em.wait("sync", scs, xt_free[par][1])
            em.dma("sync", xtbuf[par][:, :], D["x_own"][128 * i:128 * (i + 1), :], ldx[par])
            em.wait("vector", ldx[par])
            em.op("vector", lambda e, p=par: e.tensor_reduce(cols[:, 0:1], xtbuf[p][:, :], AX.X, ALU.add))
            em.op("vector", lambda e: e.tensor_single_scalar(cols[:, 0:1], cols[:, 0:1], -1.0 / 2048.0, ALU.mult), inc=v)
            em.wait("scalar", ldx[par])
            em.wait("scalar", v)
            em.wait("scalar", pe, pe.n)
            em.op("scalar", lambda e, p=par: e.activation(xq[:, :], xtbuf[p][:, :], AF.Square,
                                                          accum_out=cols[:, 1:2]), inc=scs)
            em.wait("vector", scs)
            em.op("vector", lambda e: e.tensor_single_scalar(cols[:, 1:2], cols[:, 1:2], 1.0 / 2048.0, ALU.mult))
            em.op("vector", lambda e: e.tensor_tensor(cols[:, 2:3], cols[:, 0:1], cols[:, 0:1], ALU.mult))
            em.op("vector", lambda e: e.tensor_tensor(cols[:, 1:2], cols[:, 1:2], cols[:, 2:3], ALU.subtract), inc=v)
            em.wait("scalar", v)
            em.op("scalar", lambda e: e.activation(cols[:, 3:4], cols[:, 1:2], AF.Sqrt, bias=cst[:, 2:3]), inc=scs)
            em.wait("vector", scs)
            em.op("vector", lambda e, i_=i: e.reciprocal(stats[:, 2 * i_ + 1:2 * i_ + 2], cols[:, 3:4]))
            em.op("vector", lambda e, i_=i: e.tensor_copy(stats[:, 2 * i_:2 * i_ + 1], cols[:, 0:1]))
            em.op("vector", lambda e, i_=i, p=par: e.tensor_scalar(
                xq[:, :], xtbuf[p][:, :],
                stats[:, 2 * i_:2 * i_ + 1], stats[:, 2 * i_ + 1:2 * i_ + 2],
                ALU.add, ALU.mult), inc=v)
            xt_free[par] = (v.n, scs.n)
            em.wait("tensor", v)
            if i == 0:
                em.wait("tensor", ld_id)
            for q4 in range(4):
                pst = ps_tr[q4 % 2]
                em.wait("tensor", evt, evt.n - 1)
                for j4 in range(4):
                    j = 4 * q4 + j4
                    em.op("tensor", lambda e, jj=j, p=pst, o=j4: e.transpose(
                        p[:, 128 * o:128 * (o + 1)],
                        xq[:, 128 * jj:128 * (jj + 1)], id_sb[:, :]),
                        inc=(pe if j4 == 3 else None))
                em.wait("scalar", pe)
                em.op("scalar", lambda e, i_=i, q=q4, p=pst: e.copy(
                    xnT[:, :].rearrange("p (c t) -> p c t", t=1024)[:, 4 * q:4 * (q + 1), 128 * i_:128 * (i_ + 1)],
                    p[:, 0:512].rearrange("p (c o) -> p c o", o=128)), inc=evt)
        xnT_ready = evt.n
        ph1_v, ph1_scs = v.n, scs.n

        region_last_pe = [0, 0]

        def load_quarter(ridx, name, qi):
            em.wait("sync", pe, region_last_pe[ridx])
            for ec in range(16):
                em.dma("sync", wR[ridx][:, 512 * ec:512 * (ec + 1)],
                       D[name][128 * ec:128 * (ec + 1), 512 * qi:512 * (qi + 1)], ld_r[ridx])
            return (ld_r[ridx], ld_r[ridx].n)

        def load_l0_wq_quarter(ridx, qi):
            em.wait("gpsimd", pe, region_last_pe[ridx])
            wv_ = D["wq"].rearrange("h e d -> e h d")
            for ec in range(16):
                em.dma("gpsimd",
                       wR[ridx][:, 512 * ec:512 * (ec + 1)].rearrange("p (h d) -> p h d", d=128),
                       wv_[128 * ec:128 * (ec + 1), 4 * qi:4 * (qi + 1), :], ld_r[ridx])
            return (ld_r[ridx], ld_r[ridx].n)

        def load_l0_kv(ridx, name):
            em.wait("gpsimd", pe, region_last_pe[ridx])
            wv_ = D[name].rearrange("g e d -> e g d")
            for ec in range(16):
                em.dma("gpsimd",
                       wR[ridx][:, 512 * ec:512 * (ec + 1)].rearrange("p (g d) -> p g d", d=128),
                       wv_[128 * ec:128 * (ec + 1), :, :], ld_r[ridx])
            return (ld_r[ridx], ld_r[ridx].n)

        def wait_mark(eng, mark):
            em.wait(eng, em.sem(mark[0]), mark[1])

        if layer == 0:
            em.wait("scalar", ld_rows, rows_ld)
            em.op("scalar", lambda e: e.copy(rows_bf[0:1, 0:E + 1024], rows[0:1, 0:E + 1024]), inc=scs)
        rows_bf_ready = scs.n

        ev_free = [0, 0]
        st_free = [0, 0]
        grp = [0]
        spar_ctr = [0]

        def mm_group(lhs_fn, rhs_fn, n_k, bias_lhs_rhs, evac_fn):
            p2 = grp[0] % 2
            grp[0] += 1
            em.wait("tensor", evq, ev_free[p2])
            for kk in range(n_k):
                last = (kk == n_k - 1) and bias_lhs_rhs is None
                em.op("tensor", lambda e, k=kk, p=p2, l=last: e.matmul(
                    ps_mm[p][:, 0:512], lhs_fn(k), rhs_fn(k),
                    start=(k == 0), stop=l), inc=(pe if last else None))
            if bias_lhs_rhs is not None:
                bl, br = bias_lhs_rhs
                em.op("tensor", lambda e, p=p2: e.matmul(
                    ps_mm[p][:, 0:512], bl, br, start=False, stop=True), inc=pe)
            em.wait("scalar", pe)
            evac_fn(p2)
            ev_free[p2] = evq.n

        def proj_fm(ridx, mark, out_d, oc0, noc, bias_off):
            wait_mark("tensor", mark)
            em.wait("tensor", evt, xnT_ready)
            if bias_off is not None:
                em.wait("tensor", scs, rows_bf_ready)
            for oc in range(noc):
                abs_oc = oc0 + oc
                spar = spar_ctr[0] % 2
                spar_ctr[0] += 1
                em.wait("scalar", st_s[spar], st_free[spar])
                for tb in range(2):
                    bias = None
                    if bias_off is not None:
                        bias = (rows_bf[0:1, bias_off + 128 * abs_oc:bias_off + 128 * (abs_oc + 1)],
                                ones_bf[0:1, 0:512])
                    mm_group(
                        lambda k, oc_=oc, r=ridx: wR[r][:, 512 * k + 128 * oc_:512 * k + 128 * (oc_ + 1)],
                        lambda k, t=tb: xnT[:, 1024 * k + 512 * t:1024 * k + 512 * (t + 1)],
                        16, bias,
                        lambda p2, t=tb, sp=spar: em.op("scalar", lambda e, p=p2, t_=t, s_=sp: e.copy(
                            stage[s_][:, 512 * t_:512 * (t_ + 1)], ps_mm[p][:, 0:512]), inc=evq))
                em.wait("sync", evq)
                em.dma("sync", out_d[128 * abs_oc:128 * (abs_oc + 1), :], stage[spar][:, :], st_s[spar])
                st_free[spar] = st_s[spar].n
            region_last_pe[ridx] = pe.n

        def proj_tm(ridx, mark, out_d, col0, bias_off):
            wait_mark("tensor", mark)
            em.wait("tensor", evt, xnT_ready)
            if bias_off is not None:
                em.wait("tensor", scs, rows_bf_ready)
            for i in range(8):
                spar = spar_ctr[0] % 2
                spar_ctr[0] += 1
                em.wait("scalar", st_s[spar], st_free[spar])
                bias = None
                if bias_off is not None:
                    bias = (ones_bf[0:1, 0:128], rows_bf[0:1, bias_off:bias_off + 512])
                mm_group(
                    lambda k, i_=i: xnT[:, 1024 * k + 128 * i_:1024 * k + 128 * (i_ + 1)],
                    lambda k, r=ridx: wR[r][:, 512 * k:512 * (k + 1)],
                    16, bias,
                    lambda p2, sp=spar: em.op("scalar", lambda e, p=p2, s_=sp: e.copy(
                        stage[s_][:, 0:512], ps_mm[p][:, 0:512]), inc=evq))
                em.wait("sync", evq)
                em.dma("sync", out_d[128 * i:128 * (i + 1), col0:col0 + 512], stage[spar][:, 0:512], st_s[spar])
                st_free[spar] = st_s[spar].n
            region_last_pe[ridx] = pe.n

        if layer == 0:
            m0 = load_l0_wq_quarter(0, 0)
            m1 = load_l0_wq_quarter(1, 1)
            proj_fm(0, m0, D["qfm"], 0, 4, 0)
            m2 = load_l0_wq_quarter(0, 2)
            proj_fm(1, m1, D["qfm"], 4, 4, 0)
            m3 = load_l0_wq_quarter(1, 3)
            proj_fm(0, m2, D["qfm"], 8, 4, 0)
            mk = load_l0_kv(0, "wk")
            proj_fm(1, m3, D["qfm"], 12, 4, 0)
            mv = load_l0_kv(1, "wv")
            proj_fm(0, mk, D["kfm"], 0, 4, E)
            proj_tm(1, mv, D["vtm"], 0, E + 512)
        else:
            m0 = load_quarter(0, "wqT", 0)
            m1 = load_quarter(1, "wqT", 1)
            proj_fm(0, m0, D["qfm"], 0, 4, None)
            m2 = load_quarter(0, "wqT", 2)
            proj_fm(1, m1, D["qfm"], 4, 4, None)
            m3 = load_quarter(1, "wqT", 3)
            proj_fm(0, m2, D["qfm"], 8, 4, None)
            mk0 = load_quarter(0, "wkT", 0)
            proj_fm(1, m3, D["qfm"], 12, 4, None)
            mk1 = load_quarter(1, "wkT", 1)
            proj_fm(0, mk0, D["kfm"], 0, 4, None)
            mk2 = load_quarter(0, "wkT", 2)
            proj_fm(1, mk1, D["kfm"], 4, 4, None)
            mk3 = load_quarter(1, "wkT", 3)
            proj_fm(0, mk2, D["kfm"], 8, 4, None)
            mv0 = load_quarter(0, "wvT", 0)
            proj_fm(1, mk3, D["kfm"], 12, 4, None)
            mv1 = load_quarter(1, "wvT", 1)
            proj_tm(0, mv0, D["vtm"], 0, None)
            mv2 = load_quarter(0, "wvT", 2)
            proj_tm(1, mv1, D["vtm"], 512, None)
            mv3 = load_quarter(1, "wvT", 3)
            proj_tm(0, mv2, D["vtm"], 1024, None)
            proj_tm(1, mv3, D["vtm"], 1536, None)

        qkv_pe, qkv_v, qkv_scs = pe.n, v.n, scs.n
        es2.close()
        wF = es.enter_context(nc.sbuf_tensor("wF", [128, 16 * 2048], BF16))
        wf_name = "wdqT" if layer == 0 else "kq"
        em.wait("sync", pe, qkv_pe)
        em.wait("sync", v, qkv_v)
        em.wait("sync", scs, qkv_scs)
        for ec in range(16):
            em.dma("sync", wF[:, 2048 * ec:2048 * (ec + 1)],
                   D[wf_name][128 * ec:128 * (ec + 1), :], ld_wf)
        wf_mark = ld_wf.n

        em.wait("vector", ld_rows, rows_ld)
        if layer == 0:
            em.op("vector", lambda e: e.tensor_tensor(srow[0:1, 2:3], srow[0:1, 0:1], srow[0:1, 1:2], ALU.mult))
            em.op("vector", lambda e: e.tensor_copy(srow[0:1, 3:4], srow[0:1, 2:3]))
            em.op("vector", lambda e: e.reciprocal(srow[0:1, 3:4], srow[0:1, 3:4]))
            em.op("vector", lambda e: e.tensor_single_scalar(srow[0:1, 4:5], srow[0:1, 1:2], 6.0, ALU.mult))
            em.op("vector", lambda e: e.reciprocal(srow[0:1, 4:5], srow[0:1, 4:5]))
            em.op("vector", lambda e: e.tensor_copy(srow[0:1, 5:6], srow[0:1, 1:2]))
            em.op("vector", lambda e: e.reciprocal(srow[0:1, 5:6], srow[0:1, 5:6]))
            em.op("vector", lambda e: e.tensor_scalar(rows[0:1, 3 * E:4 * E], rows[0:1, 2 * E:3 * E],
                                                      srow[0:1, 3:4], None, ALU.mult), inc=v)
            em.wait("scalar", v)
            em.op("scalar", lambda e: e.copy(ffnb_bf[0:1, :], rows[0:1, 3 * E:4 * E]), inc=scs)
            ffnb_ready = scs.n
            em.wait("tensor", v)
            em.op("tensor", lambda e: e.matmul(ps_bc[:, 0:1], ones_f[0:1, 0:128], srow[0:1, 2:3], start=True, stop=True))
            em.op("tensor", lambda e: e.matmul(ps_bc[:, 1:2], ones_f[0:1, 0:128], srow[0:1, 4:5], start=True, stop=True))
            em.op("tensor", lambda e: e.matmul(ps_bc[:, 2:3], ones_f[0:1, 0:128], srow[0:1, 5:6], start=True, stop=True), inc=pe)
            em.wait("vector", pe)
            em.op("vector", lambda e: e.tensor_copy(cols2[:, 0:3], ps_bc[:, 0:3]), inc=v)
            bc_ready = v.n
        else:
            for j in range(4):
                if j > 0:
                    em.wait("tensor", v)
                em.op("tensor", lambda e, j_=j: e.matmul(
                    ps_bc[:, 0:512], ones_f[0:1, 0:128], rows[0:1, 512 * j_:512 * (j_ + 1)],
                    start=True, stop=True), inc=pe)
                em.wait("vector", pe)
                em.op("vector", lambda e, j_=j: e.tensor_copy(skb[:, 512 * j_:512 * (j_ + 1)], ps_bc[:, 0:512]), inc=v)
            em.op("vector", lambda e: e.tensor_reduce(srow[0:1, 0:1], rows[0:1, E:2 * E], AX.X,
                                                      ALU.max, apply_absolute_value=True))
            emit_int8_cols(em, srow[0:1, 0:1], srow[0:1, 1:2], srow[0:1, 2:3],
                           srow[0:1, 3:4], srow[0:1, 4:5])
            em.op("vector", lambda e: e.tensor_scalar(rows[0:1, 2 * E:3 * E], rows[0:1, E:2 * E],
                                                      srow[0:1, 1:2], None, ALU.mult))
            emit_round(em, "vector", rows[0:1, 2 * E:3 * E], rows[0:1, 2 * E:3 * E],
                       rowi[0:1, :], rows[0:1, 3 * E:4 * E], rows[0:1, 4 * E:5 * E])
            em.op("vector", lambda e: e.tensor_scalar(rows[0:1, 2 * E:3 * E], rows[0:1, 2 * E:3 * E],
                                                      127.0, -127.0, ALU.min, ALU.max))
            em.op("vector", lambda e: e.tensor_scalar(rows[0:1, 2 * E:3 * E], rows[0:1, 2 * E:3 * E],
                                                      srow[0:1, 2:3], None, ALU.mult), inc=v)
            for j in range(4):
                em.wait("tensor", v)
                em.op("tensor", lambda e, j_=j: e.matmul(
                    ps_bc[:, 0:512], ones_f[0:1, 0:128], rows[0:1, 2 * E + 512 * j_:2 * E + 512 * (j_ + 1)],
                    start=True, stop=True), inc=pe)
                em.wait("vector", pe)
                em.op("vector", lambda e, j_=j: e.tensor_copy(bqb[:, 512 * j_:512 * (j_ + 1)], ps_bc[:, 0:512]), inc=v)
            bc_ready = v.n
            ffnb_ready = scs.n

        for i in range(8):
            em.wait("sync", v, v.n)
            em.dma("sync", xnf[:, :], D["x_own"][128 * i:128 * (i + 1), :], ld_xnf)
            em.wait("vector", ld_xnf)
            em.wait("vector", v, bc_ready)
            em.wait("vector", pe, pe.n)
            em.op("vector", lambda e, i_=i: e.tensor_scalar(
                xnf[:, :], xnf[:, :],
                stats[:, 2 * i_:2 * i_ + 1], stats[:, 2 * i_ + 1:2 * i_ + 2],
                ALU.add, ALU.mult))
            if layer == 0:
                em.op("vector", lambda e: e.tensor_reduce(
                    bamax[:, :].rearrange("p (b k) -> p b k", k=16),
                    xnf[:, :].rearrange("p (b k) -> p b k", k=16),
                    AX.X, ALU.max, apply_absolute_value=True))
                emit_e4m3(em, bamax[:, :], cols2[:, 1:2], cols2[:, 2:3], bsq, invd, e4scr, sv, ss,
                          cst[:, 0:1], cst[:, 1:2])
                emit_e2m1(em, xnf[:, :], invd, bsq, xq[:, :], qscr, sv, ss, inc=v)
            else:
                em.op("vector", lambda e: e.tensor_reduce(
                    cols[:, 0:1], xnf[:, :], AX.X, ALU.max, apply_absolute_value=True))
                emit_int8_cols(em, cols[:, 0:1], cols[:, 1:2], stats[:, 16 + i:17 + i],
                               cols[:, 2:3], cols[:, 3:4])
                em.op("vector", lambda e: e.tensor_scalar(
                    ysb[:, :], xnf[:, :], cols[:, 1:2], None, ALU.mult))
                emit_round(em, "vector", ysb[:, :], xq[:, :], i1t[:, :],
                           gsb[:, :], f2b[:, :])
                em.op("vector", lambda e: e.tensor_scalar(
                    xq[:, :], xq[:, :], 127.0, -127.0, ALU.min, ALU.max), inc=v)
            em.wait("tensor", v)
            for q4 in range(4):
                pst = ps_tr[q4 % 2]
                em.wait("tensor", evt, evt.n - 1)
                for j4 in range(4):
                    j = 4 * q4 + j4
                    em.op("tensor", lambda e, jj=j, p=pst, o=j4: e.transpose(
                        p[:, 128 * o:128 * (o + 1)], xq[:, 128 * jj:128 * (jj + 1)], id_sb[:, :]),
                        inc=(pe if j4 == 3 else None))
                em.wait("scalar", pe)
                em.op("scalar", lambda e, q=q4, p=pst: e.copy(
                    xqT[:, 512 * q:512 * (q + 1)], p[:, 0:512]), inc=evt)
            em.wait("tensor", evt)
            em.wait("tensor", ld_wf, wf_mark)
            if layer == 0:
                em.wait("tensor", scs, ffnb_ready)
            for nb in range(4):
                bias = None
                if layer == 0:
                    bias = (ones_bf[0:1, 0:128], ffnb_bf[0:1, 512 * nb:512 * (nb + 1)])
                if layer == 0:
                    evac = (lambda p2, c=512 * nb: em.op("scalar", lambda e, p=p2, c_=c: e.activation(
                        gsb[:, c_:c_ + 512], ps_mm[p][:, 0:512], AF.Silu,
                        scale=cols2[:, 0:1]), inc=evq))
                else:
                    evac = (lambda p2, c=512 * nb, i_=i: em.op("scalar", lambda e, p=p2, c_=c, ii=i_: e.activation(
                        ysb[:, c_:c_ + 512], ps_mm[p][:, 0:512], AF.Copy,
                        scale=stats[:, 16 + ii:17 + ii]), inc=evq))
                mm_group(
                    lambda k: xqT[:, 128 * k:128 * (k + 1)],
                    lambda k, n=nb: wF[:, 2048 * k + 512 * n:2048 * k + 512 * (n + 1)],
                    16, bias, evac)
            if layer == 0:
                em.wait("sync", evq)
                em.dma("sync", D["g"][128 * i:128 * (i + 1), :], gsb[:, :], st_g)
                em.wait("scalar", st_g)
                em.wait("vector", st_g)
            else:
                em.wait("vector", evq)
                em.op("vector", lambda e: e.tensor_tensor(ysb[:, :], ysb[:, :], skb[:, :], ALU.mult))
                em.op("vector", lambda e: e.tensor_tensor(ysb[:, :], ysb[:, :], bqb[:, :], ALU.add))
                em.op("vector", lambda e: e.tensor_single_scalar(ysb[:, :], ysb[:, :], 0.0, ALU.max))
                em.op("vector", lambda e: e.tensor_reduce(cols[:, 0:1], ysb[:, :], AX.X,
                                                          ALU.max, apply_absolute_value=True))
                emit_int8_cols(em, cols[:, 0:1], cols[:, 1:2], cols[:, 4:5],
                               cols[:, 2:3], cols[:, 3:4])
                em.op("vector", lambda e: e.tensor_scalar(gsb[:, :], ysb[:, :], cols[:, 1:2], None, ALU.mult))
                emit_round(em, "vector", gsb[:, :], xq[:, :], i1t[:, :],
                           ysb[:, :], f2b[:, :])
                em.op("vector", lambda e: e.tensor_scalar(xq[:, :], xq[:, :], 127.0, -127.0, ALU.min, ALU.max))
                em.op("vector", lambda e: e.tensor_scalar(ysb[:, :], xq[:, :], cols[:, 4:5], None, ALU.mult), inc=v)
                em.wait("scalar", v)
                em.op("scalar", lambda e: e.activation(gsb[:, :], ysb[:, :], AF.Silu), inc=scs)
                em.wait("sync", scs)
                em.dma("sync", D["g"][128 * i:128 * (i + 1), :], gsb[:, :], st_g)
                em.wait("scalar", st_g)
                em.wait("vector", st_g)

        em.wait("gpsimd", st_s[0])
        em.wait("gpsimd", st_s[1])
        em.wait("gpsimd", st_g)
        em.finalize()
    return nc



def build_B(layer, dbg=False):
    nc = bass.Bass("TRN2", target_bir_lowering=False)
    D = {}
    def din(n, sh, dt=F32):
        D[n] = nc.dram_tensor(n, list(sh), dt, kind="ExternalInput").ap()
    def dout(n, sh, dt=F32):
        D[n] = nc.dram_tensor(n, list(sh), dt, kind="ExternalOutput").ap()

    nh = 16
    ngr = 4 if layer == 0 else 16
    kvw = 128 * ngr
    vew = 129 * ngr
    din("x_own", (1024, E))
    din("g", (1024, E))
    din("qfm", (E, 1024), BF16)
    din("kslot", (kvw, NSLOT_TILES * 128), BF16)
    din("vslot", (NSLOT_TILES * 128, vew), BF16)
    din("maskT", (128, 128), BF16)
    if layer == 1:
        din("woT", (E, E), BF16)
        din("ident", (128, 128), BF16)
    dout("xo", (1024, E))
    if dbg:
        dout("d_l", (1024, 16))
        dout("d_s", (1024, E))

    with ExitStack() as es:
        em = Em(nc, es)
        sb = lambda n, sh, dt=F32: es.enter_context(nc.sbuf_tensor(n, sh, dt))

        qsb = sb("qsb", [128, 16 * 1024], BF16)
        ksb = [sb(f"ksb{i}", [128, kvw], BF16) for i in range(2)]
        vsb = [sb(f"vsb{i}", [128, vew], BF16) for i in range(2)]
        ptb = sb("ptb", [128, 16 * 128], BF16)
        msk = sb("msk", [128, 128], BF16)
        lsb = sb("lsb", [128, 16])
        cst = sb("cst", [128, 1])
        lrec = sb("lrec", [128, 16])
        ssb = sb("ssb", [128, E])
        xg = [sb(f"xg{i}", [128, E]) for i in range(2)]
        ost = [sb(f"ost{i}", [128, E]) for i in range(2)]
        if layer == 1:
            s_all = sb("s_all", [128, 8 * E], BF16)
            wo_sb = sb("wo_sb", [128, 16 * 2048], BF16)
            sT = sb("sT", [128, 16 * 128], BF16)
            id_sb = sb("id_sb", [128, 128], BF16)

        ldk = [em.sem("ldk0"), em.sem("ldk1")]
        ld2 = em.sem("ld2")
        wol = em.sem("wol")
        pe_sc = em.sem("pe_sc")
        exd = em.sem("exd")
        mkd = em.sem("mkd")
        av = em.sem("av")
        epv = em.sem("epv")
        eps = em.sem("eps")
        v2 = em.sem("v2")
        st = em.sem("st")
        st_o = [em.sem("st_o0"), em.sem("st_o1")]
        pe = em.sem("pe")
        evq = em.sem("evq")
        evt = em.sem("evt")

        em.dma("sync", msk[:, :], D["maskT"][:, :], ld2)
        vini = em.sem("vini")
        em.op("vector", lambda e: e.memset(cst[:, 0:1], 0.1), inc=vini)
        em.wait("scalar", vini)
        for oc in range(16):
            em.dma("sync", qsb[:, 1024 * oc:1024 * (oc + 1)], D["qfm"][128 * oc:128 * (oc + 1), :], ld2)
        if layer == 1:
            em.dma("sync", id_sb[:, :], D["ident"][:, :], ld2)
            for ec in range(16):
                em.dma("gpsimd", wo_sb[:, 2048 * ec:2048 * (ec + 1)],
                       D["woT"][128 * ec:128 * (ec + 1), :], wol)
        q_ready = ld2.n

        with ExitStack() as ps_es:
            o_ps = [ps_es.enter_context(nc.psum_tensor(f"o_ps{t}", [128, 512], F32))
                    for t in range(6)]
            ps_sc = [ps_es.enter_context(nc.psum_tensor(f"ps_sc{i}", [128, 512], F32))
                     for i in range(2)]

            def o_slice(h, w=129):
                return o_ps[h // 3][:, (h % 3) * 129:(h % 3) * 129 + w]

            kv_free = [0, 0]
            av_prev = [0]
            for j, (koff, L) in enumerate(slot_offsets()):
                sq = j // 2
                qcol = 256 * sq + (128 if j % 2 == 1 else 0)
                for kt in range(L):
                    par = kt % 2
                    em.wait("sync", av, kv_free[par])
                    abs_off = (koff + kt) * 128
                    em.dma("sync", ksb[par][:, :].rearrange("p (g k) -> p g k", k=128),
                           D["kslot"][:, abs_off:abs_off + 128].rearrange("(g p) k -> p g k", p=128), ldk[par])
                    em.dma("sync", vsb[par][:, :], D["vslot"][abs_off:abs_off + 128, :], ldk[par])
                    em.wait("tensor", ldk[par])
                    if j == 0 and kt == 0:
                        em.wait("tensor", ld2, q_ready)
                    em.wait("scalar", av, av_prev[0])
                    for h in range(nh):
                        gc = h // 4 if layer == 0 else h
                        scp = h % 2
                        em.wait("tensor", exd, exd.n - 1)
                        if kt == 0 and h == 0:
                            em.wait("tensor", eps, eps.n)
                        em.op("tensor", lambda e, g_=gc, h_=h, p=par, sp=scp, q=qcol: e.matmul(
                            ps_sc[sp][:, 0:128],
                            ksb[p][:, 128 * g_:128 * (g_ + 1)],
                            qsb[:, 1024 * h_ + q:1024 * h_ + q + 128],
                            start=True, stop=True), inc=pe_sc)
                        em.wait("scalar", pe_sc)
                        if layer == 0:
                            em.op("scalar", lambda e, h_=h, sp=scp: e.activation(
                                ptb[:, 128 * h_:128 * (h_ + 1)], ps_sc[sp][:, 0:128],
                                AF.Exp, scale=RSQ), inc=exd)
                        else:
                            em.op("scalar", lambda e, h_=h, sp=scp: e.activation(
                                ptb[:, 128 * h_:128 * (h_ + 1)], ps_sc[sp][:, 0:128],
                                AF.Relu, bias=cst[:, 0:1], scale=RSQ), inc=exd)
                        if kt == 0:
                            if j == 0 and h == 0:
                                em.wait("vector", ld2, q_ready)
                            em.wait("vector", exd)
                            em.op("vector", lambda e, h_=h: e.tensor_tensor(
                                ptb[:, 128 * h_:128 * (h_ + 1)], ptb[:, 128 * h_:128 * (h_ + 1)],
                                msk[:, :], ALU.mult), inc=mkd)
                            em.wait("tensor", mkd)
                        else:
                            em.wait("tensor", exd)
                        em.op("tensor", lambda e, h_=h, g_=gc, p=par, k=kt, l=L: e.matmul(
                            o_slice(h_),
                            ptb[:, 128 * h_:128 * (h_ + 1)],
                            vsb[p][:, 129 * g_:129 * (g_ + 1)],
                            start=(k == 0 and h_ % 3 == 0),
                            stop=(k == l - 1 and (h_ % 3 == 2 or h_ == 15))),
                            inc=(av if h == nh - 1 else None))
                    kv_free[par] = av.n
                    av_prev[0] = av.n
                em.wait("vector", av)
                for t6 in range(6):
                    nhh = 3 if t6 < 5 else 1
                    em.op("vector", lambda e, t=t6, n=nhh: e.tensor_copy(
                        lsb[:, 3 * t:3 * t + n],
                        o_ps[t][:, 0:387].rearrange("p (hh o) -> p hh o", o=129)[:, 0:n, 128:129]))
                if layer == 1:
                    em.op("vector", lambda e: e.tensor_single_scalar(lsb[:, :], lsb[:, :], 1e-6, ALU.add))
                em.op("vector", lambda e: e.reciprocal(lrec[:, :], lsb[:, :]), inc=epv)
                em.wait("scalar", epv)
                if layer == 0:
                    em.wait("scalar", v2, v2.n)
                for h in range(nh):
                    if layer == 0:
                        em.op("scalar", lambda e, h_=h: e.activation(
                            ssb[:, 128 * h_:128 * (h_ + 1)], o_slice(h_, 128), AF.Copy,
                            scale=lrec[:, h_:h_ + 1]), inc=eps)
                    else:
                        em.op("scalar", lambda e, h_=h, j_=j: e.activation(
                            s_all[:, E * j_ + 128 * h_:E * j_ + 128 * (h_ + 1)], o_slice(h_, 128),
                            AF.Copy, scale=lrec[:, h_:h_ + 1]), inc=eps)
                if dbg and layer == 0:
                    em.wait("sync", eps)
                    em.dma("sync", D["d_l"][128 * j:128 * (j + 1), :], lsb[:, :], st)
                    em.dma("sync", D["d_s"][128 * j:128 * (j + 1), :], ssb[:, :], st)
                    em.wait("vector", st)
                    em.wait("scalar", st)
                if layer == 0:
                    em.wait("sync", v2, v2.n)
                    em.dma("sync", xg[0][:, :], D["x_own"][128 * j:128 * (j + 1), :], ld2)
                    em.dma("sync", xg[1][:, :], D["g"][128 * j:128 * (j + 1), :], ld2)
                    em.wait("vector", ld2)
                    em.wait("vector", eps)
                    opar = j % 2
                    em.wait("vector", st_o[opar], 16 * (j // 2))
                    em.op("vector", lambda e, o=opar: e.tensor_tensor(ost[o][:, :], ssb[:, :], xg[1][:, :], ALU.mult))
                    em.op("vector", lambda e, o=opar: e.tensor_tensor(ost[o][:, :], ost[o][:, :], xg[0][:, :], ALU.add), inc=v2)
                    em.wait("sync", v2)
                    em.dma("sync", D["xo"][128 * j:128 * (j + 1), :], ost[opar][:, :], st_o[opar])

        if layer == 1:
            ps_tr = [es.enter_context(nc.psum_tensor(f"ps_tr{i}", [128, 1024], BF16)) for i in range(2)]
            ps_mm = [es.enter_context(nc.psum_tensor(f"ps_mm{i}", [128, 512], F32)) for i in range(2)]
            em.wait("tensor", eps)
            grp = [0]
            ev_free = [0, 0]
            for i in range(8):
                for q4 in range(4):
                    pst = ps_tr[q4 % 2]
                    em.wait("tensor", evt, evt.n - 1)
                    for j4 in range(4):
                        jj = 4 * q4 + j4
                        em.op("tensor", lambda e, i_=i, c=jj, p=pst, o=j4: e.transpose(
                            p[:, 128 * o:128 * (o + 1)],
                            s_all[:, E * i_ + 128 * c:E * i_ + 128 * (c + 1)], id_sb[:, :]),
                            inc=(pe if j4 == 3 else None))
                    em.wait("scalar", pe)
                    em.op("scalar", lambda e, q=q4, p=pst: e.copy(
                        sT[:, 512 * q:512 * (q + 1)], p[:, 0:512]), inc=evt)
                em.wait("tensor", evt)
                if i == 0:
                    em.wait("tensor", wol)
                em.wait("sync", v2, v2.n)
                em.dma("sync", xg[0][:, :], D["x_own"][128 * i:128 * (i + 1), :], ld2)
                em.dma("sync", xg[1][:, :], D["g"][128 * i:128 * (i + 1), :], ld2)
                ldxg = ld2.n
                em.wait("scalar", v2, v2.n)
                for nb in range(4):
                    p2 = grp[0] % 2
                    grp[0] += 1
                    em.wait("tensor", evq, ev_free[p2])
                    for cc in range(16):
                        em.op("tensor", lambda e, c=cc, n=nb, p=p2: e.matmul(
                            ps_mm[p][:, 0:512],
                            sT[:, 128 * c:128 * (c + 1)],
                            wo_sb[:, 2048 * c + 512 * n:2048 * c + 512 * (n + 1)],
                            start=(c == 0), stop=(c == 15)), inc=(pe if cc == 15 else None))
                    em.wait("scalar", pe)
                    em.op("scalar", lambda e, p=p2, c=512 * nb: e.copy(
                        ssb[:, c:c + 512], ps_mm[p][:, 0:512]), inc=evq)
                    ev_free[p2] = evq.n
                em.wait("vector", evq)
                em.wait("vector", ld2, ldxg)
                opar = i % 2
                em.wait("vector", st_o[opar], 16 * (i // 2))
                em.op("vector", lambda e, o=opar: e.tensor_tensor(ost[o][:, :], ssb[:, :], xg[1][:, :], ALU.mult))
                em.op("vector", lambda e, o=opar: e.tensor_tensor(ost[o][:, :], ost[o][:, :], xg[0][:, :], ALU.add), inc=v2)
                em.wait("sync", v2)
                em.dma("sync", D["xo"][128 * i:128 * (i + 1), :], ost[opar][:, :], st_o[opar])

        em.wait("gpsimd", st_o[0])
        em.wait("gpsimd", st_o[1])
        em.wait("gpsimd", st)
        em.finalize()
    return nc



_PROGS = {}


def _get_prog(key, builder, *args):
    if key not in _PROGS:
        _PROGS[key] = builder(*args)
    return _PROGS[key]


def _bf(a):
    return np.ascontiguousarray(a.astype(BF) if a.dtype != BF else a)


def _build_slots(Kseq, Vseq, c, ngr):
    kvw = 128 * ngr
    vew = 129 * ngr
    ks = np.zeros((kvw, NSLOT_TILES * 128), dtype=BF)
    vs = np.zeros((NSLOT_TILES * 128, vew), dtype=BF)
    offs = slot_offsets()
    for j, (s, w) in enumerate(own_tiles(c)):
        koff, L = offs[j]
        order = [w] + list(range(w))
        for kk, tk in enumerate(order):
            a = (koff + kk) * 128
            ks[:, a:a + 128] = Kseq[s][:, tk * 128:(tk + 1) * 128]
            for g in range(ngr):
                vs[a:a + 128, 129 * g:129 * g + 128] = Vseq[s][tk * 128:(tk + 1) * 128,
                                                               128 * g:128 * (g + 1)]
                vs[a:a + 128, 129 * g + 128] = 1.0
    return ks, vs


def kernel(L0_attn_x, L0_attn_Wq, L0_attn_bq, L0_attn_Wk, L0_attn_bk, L0_attn_Wv,
           L0_attn_bv, L0_ffn_weight, L0_ffn_bias, L0_ffn_x_tensor_scale,
           L1_attn_Wq, L1_attn_Wk, L1_attn_Wv, L1_attn_Wo, L1_ffn_kernel, L1_ffn_bias):
    x = np.ascontiguousarray(np.asarray(L0_attn_x, np.float32))
    ident = np.eye(128, dtype=BF)
    ident_f = np.eye(128, dtype=np.float32)
    maskT = np.triu(np.ones((128, 128), dtype=np.float32)).astype(BF)
    cores = list(range(8))

    w0 = np.ascontiguousarray(np.asarray(L0_ffn_weight, np.float32))
    k1 = np.ascontiguousarray(np.asarray(L1_ffn_kernel, np.float32))
    wq1 = np.ascontiguousarray(np.asarray(L1_attn_Wq, np.float32))
    wk1 = np.ascontiguousarray(np.asarray(L1_attn_Wk, np.float32))
    wv1 = np.ascontiguousarray(np.asarray(L1_attn_Wv, np.float32))
    wo1 = np.ascontiguousarray(np.asarray(L1_attn_Wo, np.float32))

    nc = _get_prog("prep", build_prep)
    im = []
    for c in cores:
        sl = slice(256 * c, 256 * (c + 1))
        im.append({"w0_full": w0, "w0_own": w0[sl], "k1_own": np.ascontiguousarray(k1[:, sl]),
                   "wq1_own": wq1[sl], "wk1_own": wk1[sl], "wv1_own": wv1[sl],
                   "wo1_own": wo1[sl], "ident": ident, "ident_f": ident_f})
    res = run_bass_kernel_spmd(nc, im, core_ids=cores).results
    wdqT = np.concatenate([r["wdqT_s"] for r in res], axis=1)
    kq = np.concatenate([r["kq_s"] for r in res], axis=1)
    kscale_row = np.concatenate([r["kscale_s"][:, 0] for r in res]).reshape(1, E).astype(np.float32)
    wqT = np.concatenate([r["wqT_s"] for r in res], axis=1)
    wkT = np.concatenate([r["wkT_s"] for r in res], axis=1)
    wvT = np.concatenate([r["wvT_s"] for r in res], axis=1)
    woT = np.concatenate([r["woT_s"] for r in res], axis=1)
    wts = np.asarray(res[0]["wts"], np.float32).reshape(1, 1)

    def gather_x(xf):
        return [np.ascontiguousarray(np.concatenate(
            [xf[s, 128 * w:128 * (w + 1)] for (s, w) in own_tiles(c)], axis=0)) for c in cores]

    def run_layer(xf, layer):
        xowns = gather_x(xf)
        ncA = _get_prog(("A", layer), build_A, layer)
        im = []
        for c in cores:
            d = {"x_own": xowns[c], "ident": ident}
            if layer == 0:
                d.update({"wq": np.ascontiguousarray(np.asarray(L0_attn_Wq, np.float32)),
                          "wk": np.ascontiguousarray(np.asarray(L0_attn_Wk, np.float32)),
                          "wv": np.ascontiguousarray(np.asarray(L0_attn_Wv, np.float32)),
                          "bq_row": np.asarray(L0_attn_bq, np.float32).reshape(1, E),
                          "bk_row": np.asarray(L0_attn_bk, np.float32).reshape(1, 512),
                          "bv_row": np.asarray(L0_attn_bv, np.float32).reshape(1, 512),
                          "wdqT": wdqT, "ffn_bias": np.asarray(L0_ffn_bias, np.float32).reshape(1, E),
                          "wts": wts,
                          "xts": np.asarray(L0_ffn_x_tensor_scale, np.float32).reshape(1, 1)})
            else:
                d.update({"wqT": wqT, "wkT": wkT, "wvT": wvT, "kq": kq,
                          "kscale_row": kscale_row,
                          "ffn_bias": np.asarray(L1_ffn_bias, np.float32).reshape(1, E)})
            im.append(d)
        resA = run_bass_kernel_spmd(ncA, im, core_ids=cores).results

        ngr = 4 if layer == 0 else 16
        kvw = 128 * ngr
        Kseq = [np.zeros((kvw, S), dtype=BF) for _ in range(B)]
        Vseq = [np.zeros((S, kvw), dtype=BF) for _ in range(B)]
        for c in cores:
            for j, (s, w) in enumerate(own_tiles(c)):
                Kseq[s][:, 128 * w:128 * (w + 1)] = resA[c]["kfm"][:, 128 * j:128 * (j + 1)]
                Vseq[s][128 * w:128 * (w + 1), :] = resA[c]["vtm"][128 * j:128 * (j + 1), :]

        ncB = _get_prog(("B", layer), build_B, layer)
        im = []
        for c in cores:
            ks, vs = _build_slots(Kseq, Vseq, c, ngr)
            d = {"x_own": xowns[c], "g": resA[c]["g"], "qfm": resA[c]["qfm"],
                 "kslot": ks, "vslot": vs, "maskT": maskT}
            if layer == 1:
                d.update({"woT": woT, "ident": ident})
            im.append(d)
        resB = run_bass_kernel_spmd(ncB, im, core_ids=cores).results

        xout = np.empty((B, S, E), np.float32)
        for c in cores:
            for j, (s, w) in enumerate(own_tiles(c)):
                xout[s, 128 * w:128 * (w + 1)] = resB[c]["xo"][128 * j:128 * (j + 1)]
        return xout

    x1 = run_layer(x, 0)
    x2 = run_layer(x1, 1)
    return x2


# version 05
# speedup vs baseline: 1.0043x; 1.0043x over previous
import sys
sys.path.insert(0, "/opt/trn_rl_repo")
from contextlib import ExitStack
import numpy as np
import ml_dtypes
import concourse.bass as bass
import concourse.mybir as mybir
from concourse.bass_utils import run_bass_kernel_spmd

F32 = mybir.dt.float32
BF16 = mybir.dt.bfloat16
I32 = mybir.dt.int32
AF = mybir.ActivationFunctionType
ALU = mybir.AluOpType
AX = mybir.AxisListType
BF = ml_dtypes.bfloat16

B, S, E = 4, 2048, 2048
RSQ = 0.08838834764831845
LN2 = 0.6931471805599453
INVLN2 = 1.4426950408889634
NSLOT_TILES = 4 * (8 + 16)


def own_tiles(c):
    return [(s, w) for s in range(B) for w in (c, 15 - c)]


def slot_offsets():
    out, off = [], 0
    for _s in range(B):
        for L in (8, 16):
            out.append((off, L))
            off += L
    return out


class SemC:
    def __init__(self, h):
        self.h = h
        self.n = 0


class Em:
    ENGS = ("sync", "gpsimd", "vector", "scalar", "tensor")

    SELF_SYNC = ("vector", "scalar")

    def __init__(self, nc, es):
        self.nc = nc
        self.es = es
        self.st = {k: [] for k in self.ENGS}
        self._sems = {}
        self._last = {}

    def sem(self, name):
        if isinstance(name, SemC):
            return name
        if name not in self._sems:
            h = self.es.enter_context(self.nc.semaphore(name))
            self._sems[name] = SemC(h)
        return self._sems[name]

    def op(self, eng, build, inc=None, k=1):
        if eng in self.SELF_SYNC:
            prev = self._last.get(eng)
            if prev is not None:
                self.st[eng].append(lambda e, h=prev[0], vv=prev[1]: e.wait_ge(h, vv))
            if inc is None:
                inc = eng + "_selfsync"
        if inc is not None:
            s = self.sem(inc)
            s.n += k
            self.st[eng].append(lambda e, b=build, h=s.h, kk=k: b(e).then_inc(h, kk))
            if eng in self.SELF_SYNC:
                self._last[eng] = (s.h, s.n)
        else:
            self.st[eng].append(lambda e, b=build: b(e))

    def dma(self, eng, out, in_, inc, **kw):
        self.op(eng, lambda e, o=out, i=in_, kw2=dict(kw): e.dma_start(out=o, in_=i, **kw2),
                inc, 16)

    def wait(self, eng, sem, val=None):
        s = self.sem(sem)
        v = s.n if val is None else val
        if v > 0:
            self.st[eng].append(lambda e, h=s.h, vv=v: e.wait_ge(h, vv))

    def finalize(self):
        with self.nc.Block() as block:
            @block.sync
            def _(e):
                for f in self.st["sync"]:
                    f(e)

            @block.gpsimd
            def _(e):
                for f in self.st["gpsimd"]:
                    f(e)

            @block.vector
            def _(e):
                for f in self.st["vector"]:
                    f(e)

            @block.scalar
            def _(e):
                for f in self.st["scalar"]:
                    f(e)

            @block.tensor
            def _(e):
                for f in self.st["tensor"]:
                    f(e)



def emit_round(em, eng, v_ap, r_out, i_scr, f1, f2, inc=None):
    em.op(eng, lambda e: e.tensor_scalar(i_scr, v_ap, 0.5, None, ALU.add))
    em.op(eng, lambda e: e.tensor_tensor(f1, v_ap, i_scr, ALU.subtract))
    em.op(eng, lambda e: e.tensor_single_scalar(f2, f1, 0.5, ALU.is_ge))
    em.op(eng, lambda e: e.tensor_single_scalar(f1, f1, -0.5, ALU.is_lt))
    em.op(eng, lambda e: e.tensor_tensor(f2, f2, f1, ALU.subtract))
    em.op(eng, lambda e: e.tensor_tensor(r_out, f2, i_scr, ALU.add), inc=inc)


def emit_e4m3(em, bamax_ap, inv6ts_col, invts_col, bsq, invd, scr, sv, ss, bm3, bp3):
    V, SC = "vector", "scalar"
    f1, f2, f3, i1 = scr["f1"][:, :], scr["f2"][:, :], scr["f3"][:, :], scr["i1"][:, :]
    bq = bsq[:, :]
    iv = invd[:, :]
    em.op(V, lambda e: e.tensor_scalar(f1, bamax_ap, inv6ts_col, None, ALU.mult))
    em.op(V, lambda e: e.tensor_scalar(f1, f1, 448.0, 2.44140625e-4, ALU.min, ALU.max), inc=sv)
    em.wait(SC, sv)
    em.op(SC, lambda e: e.activation(f2, f1, AF.Ln), inc=ss)
    em.wait(V, ss)
    em.op(V, lambda e: e.tensor_single_scalar(f2, f2, INVLN2, ALU.mult))
    em.op(V, lambda e: e.tensor_scalar(i1, f2, 0.0, None, ALU.add))
    em.op(V, lambda e: e.tensor_tensor(f3, f2, i1, ALU.subtract))
    em.op(V, lambda e: e.tensor_single_scalar(f3, f3, 0.0, ALU.is_lt))
    em.op(V, lambda e: e.tensor_tensor(f2, i1, f3, ALU.subtract))
    em.op(V, lambda e: e.tensor_scalar(f2, f2, -6.0, 8.0, ALU.max, ALU.min), inc=sv)
    em.wait(SC, sv)
    em.op(SC, lambda e: e.activation(f3, f2, AF.Exp, bias=bm3, scale=LN2))
    em.op(SC, lambda e: e.activation(f2, f2, AF.Exp, bias=bp3, scale=-LN2), inc=ss)
    em.wait(V, ss)
    em.op(V, lambda e: e.tensor_tensor(f1, f1, f2, ALU.mult))
    emit_round(em, V, f1, f1, i1, f2, bq)
    em.op(V, lambda e: e.tensor_tensor(bq, f1, f3, ALU.mult))
    em.op(V, lambda e: e.tensor_single_scalar(f2, bq, 0.0, ALU.is_equal))
    em.op(V, lambda e: e.tensor_tensor(f2, bq, f2, ALU.add))
    em.op(V, lambda e: e.reciprocal(f2, f2))
    em.op(V, lambda e: e.tensor_scalar(iv, f2, invts_col, None, ALU.mult))


def emit_e2m1(em, xin_ap, invd, bsq, xdq_ap, scr, sv, ss, inc=None):
    V, SC = "vector", "scalar"
    af, tf = scr["af"][:, :], scr["tf"][:, :]
    a2, c2, c4, f2 = (scr[k][:, :] for k in ("a2", "c2", "c4", "f2"))
    i1 = scr["i1"][:, :]

    def b3(t):
        return t[:, 0:128].rearrange("p (b one) -> p b one", one=1).broadcast_to([128, 128, 16])

    def v3(t):
        return t.rearrange("p (b k) -> p b k", k=16)

    em.op(V, lambda e: e.tensor_tensor(v3(af), v3(xin_ap), b3(invd), ALU.mult))
    em.op(V, lambda e: e.tensor_scalar(af, af, 6.0, -6.0, ALU.min, ALU.max), inc=sv)
    em.wait(SC, sv)
    em.op(SC, lambda e: e.activation(a2, af, AF.Square), inc=ss)
    em.wait(V, ss)
    em.op(V, lambda e: e.tensor_single_scalar(c2, a2, 4.0, ALU.is_lt))
    em.op(V, lambda e: e.tensor_single_scalar(c4, a2, 16.0, ALU.is_lt))
    em.op(V, lambda e: e.tensor_scalar(a2, c4, 0.5, 0.5, ALU.mult, ALU.add))
    em.op(V, lambda e: e.tensor_tensor(a2, a2, c2, ALU.add))
    em.op(V, lambda e: e.tensor_tensor(tf, af, a2, ALU.mult))
    em.op(V, lambda e: e.tensor_scalar(i1, tf, 0.5, None, ALU.add))
    em.op(V, lambda e: e.tensor_tensor(af, tf, i1, ALU.subtract))
    em.op(V, lambda e: e.tensor_single_scalar(f2, af, 0.5, ALU.is_ge))
    em.op(V, lambda e: e.tensor_single_scalar(af, af, -0.5, ALU.is_lt))
    em.op(V, lambda e: e.tensor_tensor(f2, f2, af, ALU.subtract))
    em.op(V, lambda e: e.tensor_tensor(tf, f2, i1, ALU.add))
    em.op(V, lambda e: e.tensor_scalar(c4, c4, -1.0, 2.0, ALU.mult, ALU.add))
    em.op(V, lambda e: e.scalar_tensor_tensor(c4, c2, -0.5, c4, ALU.mult, ALU.add))
    em.op(V, lambda e: e.tensor_tensor(tf, tf, c4, ALU.mult))
    em.op(V, lambda e: e.tensor_tensor(v3(xdq_ap), v3(tf), b3(bsq), ALU.mult), inc=inc)


def emit_int8_cols(em, amax_ap, inv_ap, scale_ap, s0, s1):
    V = "vector"
    em.op(V, lambda e: e.tensor_single_scalar(s0, amax_ap, 0.0, ALU.is_equal))
    em.op(V, lambda e: e.tensor_tensor(s1, amax_ap, s0, ALU.add))
    em.op(V, lambda e: e.reciprocal(s1, s1))
    em.op(V, lambda e: e.tensor_single_scalar(s1, s1, 127.0, ALU.mult))
    em.op(V, lambda e: e.scalar_tensor_tensor(inv_ap, s0, -126.0, s1, ALU.mult, ALU.add))
    em.op(V, lambda e: e.tensor_single_scalar(s1, amax_ap, 1.0 / 127.0, ALU.mult))
    em.op(V, lambda e: e.tensor_tensor(scale_ap, s1, s0, ALU.add))



def build_prep(dbg=False):
    nc = bass.Bass("TRN2", target_bir_lowering=False)
    D = {}
    def din(n, sh, dt=F32):
        D[n] = nc.dram_tensor(n, list(sh), dt, kind="ExternalInput").ap()
    def dout(n, sh, dt=F32):
        D[n] = nc.dram_tensor(n, list(sh), dt, kind="ExternalOutput").ap()

    din("w0_full", (E, E))
    din("w0_own", (256, E))
    din("k1_own", (E, 256))
    din("wq1_own", (256, E))
    din("wk1_own", (256, E))
    din("wv1_own", (256, E))
    din("wo1_own", (256, E))
    din("ident", (128, 128), BF16)
    din("ident_f", (128, 128))
    dout("wts", (1, 1))
    dout("wdqT_s", (E, 256), BF16)
    dout("kq_s", (E, 256), BF16)
    dout("kscale_s", (256, 1))
    dout("wqT_s", (E, 256), BF16)
    dout("wkT_s", (E, 256), BF16)
    dout("wvT_s", (E, 256), BF16)
    dout("woT_s", (E, 256), BF16)

    with ExitStack() as es:
        em = Em(nc, es)
        sb = lambda n, sh, dt=F32: es.enter_context(nc.sbuf_tensor(n, sh, dt))
        ps = lambda n, sh, dt=F32: es.enter_context(nc.psum_tensor(n, sh, dt))

        wf = [sb(f"wf{i}", [128, E]) for i in range(2)]
        wbf = sb("wbf", [128, E], BF16)
        bamax = sb("bamax", [128, E])
        colmax = sb("colmax", [128, 1])
        rowmax = sb("rowmax", [1, 128])
        wts_sb = sb("wts_sb", [1, 4])
        inv6ts_col = sb("inv6ts_col", [128, 1])
        invts_col = sb("invts_col", [128, 1])
        ones_f = sb("ones_f", [1, 512])
        id_sb = sb("id_sb", [128, 128], BF16)
        id_f = sb("id_f", [128, 128])
        e4scr = {"f1": sb("e4_f1", [128, 128]), "f2": sb("e4_f2", [128, 128]),
                 "f3": sb("e4_f3", [128, 128]), "i1": sb("e4_i1", [128, 128], I32)}
        pf_a = sb("pf_a", [128, E])
        pf_t = sb("pf_t", [128, E])
        qscr = {k: sb("q_" + k, [128, E], BF16) for k in ("a2", "c2", "c4", "f2")}
        qscr["af"] = pf_a
        qscr["tf"] = pf_t
        qscr["i1"] = sb("q_i1", [128, E], I32)
        bsq = sb("bsq", [128, 128])
        invd = sb("invd", [128, 128])
        wdq = sb("wdq", [128, E], BF16)
        stage = sb("stage", [128, 16 * 256], BF16)
        kT = [sb(f"kT{i}", [128, E]) for i in range(2)]
        kqb = sb("kqb", [128, E], BF16)
        kamax = sb("kamax", [128, 2])
        kinv = sb("kinv", [128, 2])
        kscale = sb("kscale", [128, 2])
        small = sb("small", [128, 8])
        cst = sb("cst", [128, 2])
        ps_t = [ps(f"ps_t{i}", [128, 1024], BF16) for i in range(2)]
        ps_tf = [ps(f"ps_tf{i}", [128, 512]) for i in range(2)]
        ps_c = ps("ps_c", [128, 512])

        gld = em.sem("gld")
        ld_id = em.sem("ld_id")
        ld_w0 = em.sem("ld_w0")
        ld_w1 = em.sem("ld_w1")
        v = em.sem("v")
        sc = em.sem("sc")
        pe = em.sem("pe")
        st = em.sem("st")
        sv = em.sem("sv")
        ss = em.sem("ss")

        em.dma("sync", id_sb[:, :], D["ident"][:, :], ld_id)
        em.dma("sync", id_f[:, :], D["ident_f"][:, :], ld_id)
        em.op("vector", lambda e: e.memset(ones_f[:, :], 1.0))
        em.op("vector", lambda e: e.memset(cst[:, 0:1], -3.0 * LN2))
        em.op("vector", lambda e: e.memset(cst[:, 1:2], 3.0 * LN2), inc=v)
        em.wait("scalar", v)

        vmarks = []
        for ch in range(16):
            par = ch % 2
            if ch >= 2:
                em.wait("sync", v, vmarks[ch - 2])
            em.dma("sync", wf[par][:, :], D["w0_full"][128 * ch:128 * (ch + 1), :],
                   ld_w0 if par == 0 else ld_w1)
            em.wait("vector", ld_w0 if par == 0 else ld_w1)
            em.op("vector", lambda e, c=ch, p=par: e.tensor_reduce(
                bamax[:, 128 * c:128 * (c + 1)].rearrange("p (b k) -> p b k", k=16),
                wf[p][:, :].rearrange("p (b k) -> p b k", k=16),
                AX.X, ALU.max, apply_absolute_value=True), inc=v)
            vmarks.append(v.n)
        em.op("vector", lambda e: e.tensor_reduce(colmax[:, 0:1], bamax[:, :], AX.X,
                                                  ALU.max, apply_absolute_value=True), inc=v)
        em.wait("tensor", v)
        em.wait("tensor", ld_id)
        em.op("tensor", lambda e: e.transpose(ps_tf[0][0:1, 0:128], colmax[:, 0:1], id_f[:, :]), inc=pe)
        em.wait("vector", pe)
        em.op("vector", lambda e: e.tensor_copy(rowmax[0:1, :], ps_tf[0][0:1, 0:128]))
        em.op("vector", lambda e: e.tensor_reduce(wts_sb[0:1, 0:1], rowmax[0:1, :], AX.X,
                                                  ALU.max, apply_absolute_value=True))
        em.op("vector", lambda e: e.tensor_single_scalar(wts_sb[0:1, 0:1], wts_sb[0:1, 0:1],
                                                         1.0 / 2688.0, ALU.mult))
        em.op("vector", lambda e: e.tensor_single_scalar(wts_sb[0:1, 1:2], wts_sb[0:1, 0:1],
                                                         6.0, ALU.mult))
        em.op("vector", lambda e: e.reciprocal(wts_sb[0:1, 1:2], wts_sb[0:1, 1:2]))
        em.op("vector", lambda e: e.reciprocal(wts_sb[0:1, 2:3], wts_sb[0:1, 0:1]), inc=v)
        em.wait("sync", v)
        em.dma("sync", D["wts"][:, :], wts_sb[0:1, 0:1], st)
        em.wait("tensor", v)
        em.op("tensor", lambda e: e.matmul(ps_c[:, 0:1], ones_f[0:1, 0:128], wts_sb[0:1, 1:2],
                                           start=True, stop=True))
        em.op("tensor", lambda e: e.matmul(ps_c[:, 1:2], ones_f[0:1, 0:128], wts_sb[0:1, 2:3],
                                           start=True, stop=True), inc=pe)
        em.wait("vector", pe)
        em.op("vector", lambda e: e.tensor_copy(inv6ts_col[:, 0:1], ps_c[:, 0:1]))
        em.op("vector", lambda e: e.tensor_copy(invts_col[:, 0:1], ps_c[:, 1:2]), inc=v)

        for cc in range(2):
            em.wait("sync", v, v.n)
            em.dma("sync", wf[0][:, :], D["w0_own"][128 * cc:128 * (cc + 1), :], ld_w0)
            em.wait("vector", ld_w0)
            em.op("vector", lambda e: e.tensor_reduce(
                bamax[:, 0:128].rearrange("p (b k) -> p b k", k=16),
                wf[0][:, :].rearrange("p (b k) -> p b k", k=16),
                AX.X, ALU.max, apply_absolute_value=True))
            emit_e4m3(em, bamax[:, 0:128], inv6ts_col[:, 0:1], invts_col[:, 0:1],
                      bsq, invd, e4scr, sv, ss, cst[:, 0:1], cst[:, 1:2])
            emit_e2m1(em, wf[0][:, :], invd, bsq, wdq[:, :], qscr, sv, ss, inc=v)
            em.wait("tensor", v)
            for q4 in range(4):
                pst = ps_t[q4 % 2]
                em.wait("tensor", sc, sc.n - 1)
                for j4 in range(4):
                    j = 4 * q4 + j4
                    em.op("tensor", lambda e, jj=j, p=pst, o=j4: e.transpose(
                        p[:, 128 * o:128 * (o + 1)], wdq[:, 128 * jj:128 * (jj + 1)],
                        id_sb[:, :]), inc=(pe if j4 == 3 else None))
                em.wait("scalar", pe)
                em.op("scalar", lambda e, q=q4, c=cc, p=pst: e.copy(
                    stage[:, :].rearrange("p (kj n) -> p kj n", n=256)[:, 4 * q:4 * (q + 1), 128 * c:128 * (c + 1)],
                    p[:, 0:512].rearrange("p (j o) -> p j o", o=128)), inc=sc)
        em.wait("sync", sc)
        for kj in range(16):
            em.dma("sync", D["wdqT_s"][128 * kj:128 * (kj + 1), :],
                   stage[:, 256 * kj:256 * (kj + 1)], st)

        em.wait("sync", v, v.n)
        em.wait("sync", pe, pe.n)
        for dc in range(16):
            em.wait("sync", pe, pe.n)
            em.dma("sync", wf[1][:, 0:256], D["k1_own"][128 * dc:128 * (dc + 1), :], ld_w1)
            em.wait("tensor", ld_w1)
            em.wait("tensor", sc, sc.n)
            for fb in range(2):
                em.op("tensor", lambda e, f=fb: e.transpose(
                    ps_tf[f][:, 0:128], wf[1][:, 128 * f:128 * (f + 1)], id_f[:, :]),
                    inc=(pe if fb == 1 else None))
            em.wait("scalar", pe)
            for fb in range(2):
                em.op("scalar", lambda e, f=fb, d=dc: e.copy(
                    kT[f][:, 128 * d:128 * (d + 1)], ps_tf[f][:, 0:128]), inc=sc)
        em.wait("vector", sc)
        for fb in range(2):
            em.op("vector", lambda e, f=fb: e.tensor_reduce(
                kamax[:, f:f + 1], kT[f][:, :], AX.X, ALU.max, apply_absolute_value=True))
            emit_int8_cols(em, kamax[:, fb:fb + 1], kinv[:, fb:fb + 1], kscale[:, fb:fb + 1],
                           small[:, 0:1], small[:, 1:2])
            em.op("vector", lambda e, f=fb: e.tensor_scalar(
                pf_a[:, :], kT[f][:, :], kinv[:, f:f + 1], None, ALU.mult))
            emit_round(em, "vector", pf_a[:, :], pf_t[:, :], qscr["i1"][:, :],
                       wf[0][:, :], qscr["f2"][:, :])
            em.op("vector", lambda e: e.tensor_scalar(pf_t[:, :], pf_t[:, :],
                                                      127.0, -127.0, ALU.min, ALU.max))
            em.op("vector", lambda e, f=fb: e.tensor_copy(kT[f][:, :], pf_t[:, :]), inc=v)
        em.wait("sync", v)
        em.dma("sync", D["kscale_s"][0:128, :], kscale[:, 0:1], st)
        em.dma("sync", D["kscale_s"][128:256, :], kscale[:, 1:2], st)
        em.wait("scalar", v)
        em.wait("scalar", st, st.n)
        em.wait("tensor", v)
        for dj in range(16):
            em.wait("scalar", pe, pe.n)
            for fb in range(2):
                em.op("scalar", lambda e, d=dj, f=fb: e.copy(
                    kqb[:, 128 * f:128 * (f + 1)], kT[f][:, 128 * d:128 * (d + 1)]), inc=sc)
            em.wait("tensor", sc)
            for fb in range(2):
                em.op("tensor", lambda e, f=fb: e.transpose(
                    ps_t[f][:, 0:128], kqb[:, 128 * f:128 * (f + 1)], id_sb[:, :]),
                    inc=(pe if fb == 1 else None))
            em.wait("scalar", pe)
            for fb in range(2):
                em.op("scalar", lambda e, d=dj, f=fb: e.copy(
                    stage[:, 256 * d + 128 * f:256 * d + 128 * (f + 1)], ps_t[f][:, 0:128]), inc=sc)
        em.wait("sync", sc)
        for dj in range(16):
            em.dma("sync", D["kq_s"][128 * dj:128 * (dj + 1), :],
                   stage[:, 256 * dj:256 * (dj + 1)], st)

        for wname, oname in (("wq1_own", "wqT_s"), ("wk1_own", "wkT_s"),
                             ("wv1_own", "wvT_s"), ("wo1_own", "woT_s")):
            em.wait("scalar", st, st.n)
            for oc in range(2):
                em.wait("gpsimd", pe, pe.n)
                em.dma("gpsimd", wbf[:, :], D[wname][128 * oc:128 * (oc + 1), :], gld)
                em.wait("tensor", gld)
                for cj in range(16):
                    par = cj % 2
                    em.wait("tensor", sc, sc.n - 1)
                    em.op("tensor", lambda e, c=cj, p=par: e.transpose(
                        ps_t[p][:, 0:128], wbf[:, 128 * c:128 * (c + 1)], id_sb[:, :]), inc=pe)
                    em.wait("scalar", pe)
                    em.op("scalar", lambda e, c=cj, o=oc, p=par: e.copy(
                        stage[:, 256 * c + 128 * o:256 * c + 128 * (o + 1)], ps_t[p][:, 0:128]), inc=sc)
            em.wait("sync", sc)
            for cj in range(16):
                em.dma("sync", D[oname][128 * cj:128 * (cj + 1), :],
                       stage[:, 256 * cj:256 * (cj + 1)], st)

        if dbg:
            for nm, sh in (("d_bamax", (128, E)), ("d_colmax", (128, 1)),
                           ("d_rowmax", (1, 128)), ("d_wts4", (1, 4)),
                           ("d_kamax", (128, 2)), ("d_i6", (128, 1)), ("d_it", (128, 1)),
                           ("d_bsq", (128, 128)), ("d_invd", (128, 128))):
                dout(nm, sh)
            em.wait("sync", v)
            em.wait("sync", sc)
            em.wait("sync", pe)
            em.dma("sync", D["d_bamax"][:, :], bamax[:, :], st)
            em.dma("sync", D["d_colmax"][:, :], colmax[:, :], st)
            em.dma("sync", D["d_rowmax"][:, :], rowmax[0:1, :], st)
            em.dma("sync", D["d_wts4"][:, :], wts_sb[0:1, :], st)
            em.dma("sync", D["d_kamax"][:, :], kamax[:, :], st)
            em.dma("sync", D["d_i6"][:, :], inv6ts_col[:, :], st)
            em.dma("sync", D["d_it"][:, :], invts_col[:, :], st)
            em.dma("sync", D["d_bsq"][:, :], bsq[:, :], st)
            em.dma("sync", D["d_invd"][:, :], invd[:, :], st)
        em.wait("gpsimd", st)
        em.finalize()
    return nc



def build_A(layer):
    nc = bass.Bass("TRN2", target_bir_lowering=False)
    D = {}
    def din(n, sh, dt=F32):
        D[n] = nc.dram_tensor(n, list(sh), dt, kind="ExternalInput").ap()
    def dout(n, sh, dt=F32):
        D[n] = nc.dram_tensor(n, list(sh), dt, kind="ExternalOutput").ap()

    kvw = 512 if layer == 0 else 2048
    din("x_own", (1024, E))
    din("ident", (128, 128), BF16)
    if layer == 0:
        din("wq", (16, E, 128))
        din("wk", (4, E, 128))
        din("wv", (4, E, 128))
        din("bq_row", (1, E))
        din("bk_row", (1, 512))
        din("bv_row", (1, 512))
        din("wdqT", (E, E), BF16)
        din("ffn_bias", (1, E))
        din("wts", (1, 1))
        din("xts", (1, 1))
    else:
        din("wqT", (E, E), BF16)
        din("wkT", (E, E), BF16)
        din("wvT", (E, E), BF16)
        din("kq", (E, E), BF16)
        din("kscale_row", (1, E))
        din("ffn_bias", (1, E))
    dout("qfm", (E, 1024), BF16)
    dout("kfm", (kvw, 1024), BF16)
    dout("vtm", (1024, kvw), BF16)
    dout("g", (1024, E))

    with ExitStack() as es:
        em = Em(nc, es)
        sb = lambda n, sh, dt=F32: es.enter_context(nc.sbuf_tensor(n, sh, dt))
        ps = lambda n, sh, dt=F32: es.enter_context(nc.psum_tensor(n, sh, dt))

        gsb = sb("gsb", [128, E])
        xnf = sb("xnf", [128, E])
        xq = sb("xq", [128, E], BF16)
        xqT = sb("xqT", [128, 16 * 128], BF16)
        stage = [sb(f"stage{i}", [128, 1024], BF16) for i in range(2)]
        cols = sb("cols", [128, 8])
        cols2 = sb("cols2", [128, 4])
        stats = sb("stats", [128, 32])
        srow = sb("srow", [1, 8])
        ones_bf = sb("ones_bf", [1, 512], BF16)
        ones_f = sb("ones_f", [1, 512])
        id_sb = sb("id_sb", [128, 128], BF16)
        ffnb_bf = sb("ffnb_bf", [1, E], BF16)
        cst = sb("cst", [128, 3])
        rows = sb("rows", [1, 5 * E])
        rowi = sb("rowi", [1, E], I32)
        i1t = sb("i1t", [128, E], I32)
        f2b = sb("f2b", [128, E], BF16)
        if layer == 0:
            QA = sb("QA", [128, E])
            QT = sb("QT", [128, E])
            qscr = {"af": QA, "tf": QT, "i1": i1t, "f2": f2b,
                    "a2": sb("q_a2", [128, E], BF16),
                    "c2": sb("q_c2", [128, E], BF16),
                    "c4": sb("q_c4", [128, E], BF16)}
            bamax = sb("bamax", [128, 128])
            bsq = sb("bsq", [128, 128])
            invd = sb("invd", [128, 128])
            e4scr = {"f1": sb("e4_f1", [128, 128]), "f2": sb("e4_f2", [128, 128]),
                     "f3": sb("e4_f3", [128, 128]), "i1": sb("e4_i1", [128, 128], I32)}
            xtbuf = [gsb, QA]
        else:
            ysb = sb("ysb", [128, E])
            skb = sb("skb", [128, E])
            bqb = sb("bqb", [128, E])
            xtbuf = [gsb, ysb]

        ps_mm = [ps(f"ps_mm{i}", [128, 512]) for i in range(2)]
        ps_tr = [ps(f"ps_tr{i}", [128, 1024], BF16) for i in range(2)]
        ps_bc = ps("ps_bc", [128, 512])

        ld_id = em.sem("ld_id")
        ld_rows = em.sem("ld_rows")
        ldx = [em.sem("ldx0"), em.sem("ldx1")]
        ld_r = [em.sem("ld_r0"), em.sem("ld_r1")]
        ld_wf = em.sem("ld_wf")
        ld_xnf = em.sem("ld_xnf")
        gld = em.sem("gld")
        v = em.sem("v")
        scs = em.sem("scs")
        pe = em.sem("pe")
        evq = em.sem("evq")
        evt = em.sem("evt")
        st_s = [em.sem("st_s0"), em.sem("st_s1")]
        st_g = em.sem("st_g")
        sv = em.sem("sv")
        ss = em.sem("ss")

        em.dma("sync", id_sb[:, :], D["ident"][:, :], ld_id)
        em.op("vector", lambda e: e.memset(ones_bf[:, :], 1.0))
        em.op("vector", lambda e: e.memset(ones_f[:, :], 1.0))
        em.op("vector", lambda e: e.memset(cst[:, 0:1], -3.0 * LN2))
        em.op("vector", lambda e: e.memset(cst[:, 1:2], 3.0 * LN2))
        em.op("vector", lambda e: e.memset(cst[:, 2:3], 1e-5), inc=v)
        em.wait("scalar", v)

        if layer == 0:
            em.dma("sync", rows[0:1, 0:E], D["bq_row"][:, :], ld_rows)
            em.dma("sync", rows[0:1, E:E + 512], D["bk_row"][:, :], ld_rows)
            em.dma("sync", rows[0:1, E + 512:E + 1024], D["bv_row"][:, :], ld_rows)
            em.dma("sync", rows[0:1, 2 * E:3 * E], D["ffn_bias"][:, :], ld_rows)
            em.dma("sync", srow[0:1, 0:1], D["wts"][:, :], ld_rows)
            em.dma("sync", srow[0:1, 1:2], D["xts"][:, :], ld_rows)
        else:
            em.dma("sync", rows[0:1, 0:E], D["kscale_row"][:, :], ld_rows)
            em.dma("sync", rows[0:1, E:2 * E], D["ffn_bias"][:, :], ld_rows)
        rows_ld = ld_rows.n

        es2 = ExitStack()
        xnT = es2.enter_context(nc.sbuf_tensor("xnT", [128, 16 * 1024], BF16))
        rows_bf = es2.enter_context(nc.sbuf_tensor("rows_bf", [1, 2 * E], BF16))
        wR = [es2.enter_context(nc.sbuf_tensor(f"wR{i}", [128, 16 * 512], BF16)) for i in range(2)]

        xt_free = [(0, 0), (0, 0)]
        for i in range(8):
            par = i % 2
            em.wait("sync", v, xt_free[par][0])
            em.wait("sync", scs, xt_free[par][1])
            em.dma("sync", xtbuf[par][:, :], D["x_own"][128 * i:128 * (i + 1), :], ldx[par])
            em.wait("vector", ldx[par])
            em.op("vector", lambda e, p=par: e.tensor_reduce(cols[:, 0:1], xtbuf[p][:, :], AX.X, ALU.add))
            em.op("vector", lambda e: e.tensor_single_scalar(cols[:, 0:1], cols[:, 0:1], -1.0 / 2048.0, ALU.mult), inc=v)
            em.wait("scalar", ldx[par])
            em.wait("scalar", v)
            em.wait("scalar", pe, pe.n)
            em.op("scalar", lambda e, p=par: e.activation(xq[:, :], xtbuf[p][:, :], AF.Square,
                                                          accum_out=cols[:, 1:2]), inc=scs)
            em.wait("vector", scs)
            em.op("vector", lambda e: e.tensor_single_scalar(cols[:, 1:2], cols[:, 1:2], 1.0 / 2048.0, ALU.mult))
            em.op("vector", lambda e: e.tensor_tensor(cols[:, 2:3], cols[:, 0:1], cols[:, 0:1], ALU.mult))
            em.op("vector", lambda e: e.tensor_tensor(cols[:, 1:2], cols[:, 1:2], cols[:, 2:3], ALU.subtract), inc=v)
            em.wait("scalar", v)
            em.op("scalar", lambda e: e.activation(cols[:, 3:4], cols[:, 1:2], AF.Sqrt, bias=cst[:, 2:3]), inc=scs)
            em.wait("vector", scs)
            em.op("vector", lambda e, i_=i: e.reciprocal(stats[:, 2 * i_ + 1:2 * i_ + 2], cols[:, 3:4]))
            em.op("vector", lambda e, i_=i: e.tensor_copy(stats[:, 2 * i_:2 * i_ + 1], cols[:, 0:1]))
            em.op("vector", lambda e, i_=i, p=par: e.tensor_scalar(
                xq[:, :], xtbuf[p][:, :],
                stats[:, 2 * i_:2 * i_ + 1], stats[:, 2 * i_ + 1:2 * i_ + 2],
                ALU.add, ALU.mult), inc=v)
            xt_free[par] = (v.n, scs.n)
            em.wait("tensor", v)
            if i == 0:
                em.wait("tensor", ld_id)
            for q4 in range(4):
                pst = ps_tr[q4 % 2]
                em.wait("tensor", evt, evt.n - 1)
                for j4 in range(4):
                    j = 4 * q4 + j4
                    em.op("tensor", lambda e, jj=j, p=pst, o=j4: e.transpose(
                        p[:, 128 * o:128 * (o + 1)],
                        xq[:, 128 * jj:128 * (jj + 1)], id_sb[:, :]),
                        inc=(pe if j4 == 3 else None))
                em.wait("scalar", pe)
                em.op("scalar", lambda e, i_=i, q=q4, p=pst: e.copy(
                    xnT[:, :].rearrange("p (c t) -> p c t", t=1024)[:, 4 * q:4 * (q + 1), 128 * i_:128 * (i_ + 1)],
                    p[:, 0:512].rearrange("p (c o) -> p c o", o=128)), inc=evt)
        xnT_ready = evt.n
        ph1_v, ph1_scs = v.n, scs.n

        region_last_pe = [0, 0]

        def load_quarter(ridx, name, qi):
            em.wait("sync", pe, region_last_pe[ridx])
            for ec in range(16):
                em.dma("sync", wR[ridx][:, 512 * ec:512 * (ec + 1)],
                       D[name][128 * ec:128 * (ec + 1), 512 * qi:512 * (qi + 1)], ld_r[ridx])
            return (ld_r[ridx], ld_r[ridx].n)

        def load_l0_wq_quarter(ridx, qi):
            em.wait("gpsimd", pe, region_last_pe[ridx])
            wv_ = D["wq"].rearrange("h e d -> e h d")
            for ec in range(16):
                em.dma("gpsimd",
                       wR[ridx][:, 512 * ec:512 * (ec + 1)].rearrange("p (h d) -> p h d", d=128),
                       wv_[128 * ec:128 * (ec + 1), 4 * qi:4 * (qi + 1), :], ld_r[ridx])
            return (ld_r[ridx], ld_r[ridx].n)

        def load_l0_kv(ridx, name):
            em.wait("gpsimd", pe, region_last_pe[ridx])
            wv_ = D[name].rearrange("g e d -> e g d")
            for ec in range(16):
                em.dma("gpsimd",
                       wR[ridx][:, 512 * ec:512 * (ec + 1)].rearrange("p (g d) -> p g d", d=128),
                       wv_[128 * ec:128 * (ec + 1), :, :], ld_r[ridx])
            return (ld_r[ridx], ld_r[ridx].n)

        def wait_mark(eng, mark):
            em.wait(eng, em.sem(mark[0]), mark[1])

        if layer == 0:
            em.wait("scalar", ld_rows, rows_ld)
            em.op("scalar", lambda e: e.copy(rows_bf[0:1, 0:E + 1024], rows[0:1, 0:E + 1024]), inc=scs)
        rows_bf_ready = scs.n

        ev_free = [0, 0]
        st_free = [0, 0]
        grp = [0]
        spar_ctr = [0]

        def mm_group(lhs_fn, rhs_fn, n_k, bias_lhs_rhs, evac_fn):
            p2 = grp[0] % 2
            grp[0] += 1
            em.wait("tensor", evq, ev_free[p2])
            for kk in range(n_k):
                last = (kk == n_k - 1) and bias_lhs_rhs is None
                em.op("tensor", lambda e, k=kk, p=p2, l=last: e.matmul(
                    ps_mm[p][:, 0:512], lhs_fn(k), rhs_fn(k),
                    start=(k == 0), stop=l), inc=(pe if last else None))
            if bias_lhs_rhs is not None:
                bl, br = bias_lhs_rhs
                em.op("tensor", lambda e, p=p2: e.matmul(
                    ps_mm[p][:, 0:512], bl, br, start=False, stop=True), inc=pe)
            em.wait("scalar", pe)
            evac_fn(p2)
            ev_free[p2] = evq.n

        def proj_fm(ridx, mark, out_d, oc0, noc, bias_off):
            wait_mark("tensor", mark)
            em.wait("tensor", evt, xnT_ready)
            if bias_off is not None:
                em.wait("tensor", scs, rows_bf_ready)
            for oc in range(noc):
                abs_oc = oc0 + oc
                spar = spar_ctr[0] % 2
                spar_ctr[0] += 1
                em.wait("scalar", st_s[spar], st_free[spar])
                for tb in range(2):
                    bias = None
                    if bias_off is not None:
                        bias = (rows_bf[0:1, bias_off + 128 * abs_oc:bias_off + 128 * (abs_oc + 1)],
                                ones_bf[0:1, 0:512])
                    mm_group(
                        lambda k, oc_=oc, r=ridx: wR[r][:, 512 * k + 128 * oc_:512 * k + 128 * (oc_ + 1)],
                        lambda k, t=tb: xnT[:, 1024 * k + 512 * t:1024 * k + 512 * (t + 1)],
                        16, bias,
                        lambda p2, t=tb, sp=spar: em.op("scalar", lambda e, p=p2, t_=t, s_=sp: e.copy(
                            stage[s_][:, 512 * t_:512 * (t_ + 1)], ps_mm[p][:, 0:512]), inc=evq))
                em.wait("sync", evq)
                em.dma("sync", out_d[128 * abs_oc:128 * (abs_oc + 1), :], stage[spar][:, :], st_s[spar])
                st_free[spar] = st_s[spar].n
            region_last_pe[ridx] = pe.n

        def proj_tm(ridx, mark, out_d, col0, bias_off):
            wait_mark("tensor", mark)
            em.wait("tensor", evt, xnT_ready)
            if bias_off is not None:
                em.wait("tensor", scs, rows_bf_ready)
            for i in range(8):
                spar = spar_ctr[0] % 2
                spar_ctr[0] += 1
                em.wait("scalar", st_s[spar], st_free[spar])
                bias = None
                if bias_off is not None:
                    bias = (ones_bf[0:1, 0:128], rows_bf[0:1, bias_off:bias_off + 512])
                mm_group(
                    lambda k, i_=i: xnT[:, 1024 * k + 128 * i_:1024 * k + 128 * (i_ + 1)],
                    lambda k, r=ridx: wR[r][:, 512 * k:512 * (k + 1)],
                    16, bias,
                    lambda p2, sp=spar: em.op("scalar", lambda e, p=p2, s_=sp: e.copy(
                        stage[s_][:, 0:512], ps_mm[p][:, 0:512]), inc=evq))
                em.wait("sync", evq)
                em.dma("sync", out_d[128 * i:128 * (i + 1), col0:col0 + 512], stage[spar][:, 0:512], st_s[spar])
                st_free[spar] = st_s[spar].n
            region_last_pe[ridx] = pe.n

        if layer == 0:
            m0 = load_l0_wq_quarter(0, 0)
            m1 = load_l0_wq_quarter(1, 1)
            proj_fm(0, m0, D["qfm"], 0, 4, 0)
            m2 = load_l0_wq_quarter(0, 2)
            proj_fm(1, m1, D["qfm"], 4, 4, 0)
            m3 = load_l0_wq_quarter(1, 3)
            proj_fm(0, m2, D["qfm"], 8, 4, 0)
            mk = load_l0_kv(0, "wk")
            proj_fm(1, m3, D["qfm"], 12, 4, 0)
            mv = load_l0_kv(1, "wv")
            proj_fm(0, mk, D["kfm"], 0, 4, E)
            proj_tm(1, mv, D["vtm"], 0, E + 512)
        else:
            m0 = load_quarter(0, "wqT", 0)
            m1 = load_quarter(1, "wqT", 1)
            proj_fm(0, m0, D["qfm"], 0, 4, None)
            m2 = load_quarter(0, "wqT", 2)
            proj_fm(1, m1, D["qfm"], 4, 4, None)
            m3 = load_quarter(1, "wqT", 3)
            proj_fm(0, m2, D["qfm"], 8, 4, None)
            mk0 = load_quarter(0, "wkT", 0)
            proj_fm(1, m3, D["qfm"], 12, 4, None)
            mk1 = load_quarter(1, "wkT", 1)
            proj_fm(0, mk0, D["kfm"], 0, 4, None)
            mk2 = load_quarter(0, "wkT", 2)
            proj_fm(1, mk1, D["kfm"], 4, 4, None)
            mk3 = load_quarter(1, "wkT", 3)
            proj_fm(0, mk2, D["kfm"], 8, 4, None)
            mv0 = load_quarter(0, "wvT", 0)
            proj_fm(1, mk3, D["kfm"], 12, 4, None)
            mv1 = load_quarter(1, "wvT", 1)
            proj_tm(0, mv0, D["vtm"], 0, None)
            mv2 = load_quarter(0, "wvT", 2)
            proj_tm(1, mv1, D["vtm"], 512, None)
            mv3 = load_quarter(1, "wvT", 3)
            proj_tm(0, mv2, D["vtm"], 1024, None)
            proj_tm(1, mv3, D["vtm"], 1536, None)

        qkv_pe, qkv_v, qkv_scs = pe.n, v.n, scs.n
        es2.close()
        wF = es.enter_context(nc.sbuf_tensor("wF", [128, 16 * 2048], BF16))
        wf_name = "wdqT" if layer == 0 else "kq"
        em.wait("sync", pe, qkv_pe)
        em.wait("sync", v, qkv_v)
        em.wait("sync", scs, qkv_scs)
        for ec in range(16):
            em.dma("sync", wF[:, 2048 * ec:2048 * (ec + 1)],
                   D[wf_name][128 * ec:128 * (ec + 1), :], ld_wf)
        wf_mark = ld_wf.n

        em.wait("vector", ld_rows, rows_ld)
        if layer == 0:
            em.op("vector", lambda e: e.tensor_tensor(srow[0:1, 2:3], srow[0:1, 0:1], srow[0:1, 1:2], ALU.mult))
            em.op("vector", lambda e: e.tensor_copy(srow[0:1, 3:4], srow[0:1, 2:3]))
            em.op("vector", lambda e: e.reciprocal(srow[0:1, 3:4], srow[0:1, 3:4]))
            em.op("vector", lambda e: e.tensor_single_scalar(srow[0:1, 4:5], srow[0:1, 1:2], 6.0, ALU.mult))
            em.op("vector", lambda e: e.reciprocal(srow[0:1, 4:5], srow[0:1, 4:5]))
            em.op("vector", lambda e: e.tensor_copy(srow[0:1, 5:6], srow[0:1, 1:2]))
            em.op("vector", lambda e: e.reciprocal(srow[0:1, 5:6], srow[0:1, 5:6]))
            em.op("vector", lambda e: e.tensor_scalar(rows[0:1, 3 * E:4 * E], rows[0:1, 2 * E:3 * E],
                                                      srow[0:1, 3:4], None, ALU.mult), inc=v)
            em.wait("scalar", v)
            em.op("scalar", lambda e: e.copy(ffnb_bf[0:1, :], rows[0:1, 3 * E:4 * E]), inc=scs)
            ffnb_ready = scs.n
            em.wait("tensor", v)
            em.op("tensor", lambda e: e.matmul(ps_bc[:, 0:1], ones_f[0:1, 0:128], srow[0:1, 2:3], start=True, stop=True))
            em.op("tensor", lambda e: e.matmul(ps_bc[:, 1:2], ones_f[0:1, 0:128], srow[0:1, 4:5], start=True, stop=True))
            em.op("tensor", lambda e: e.matmul(ps_bc[:, 2:3], ones_f[0:1, 0:128], srow[0:1, 5:6], start=True, stop=True), inc=pe)
            em.wait("vector", pe)
            em.op("vector", lambda e: e.tensor_copy(cols2[:, 0:3], ps_bc[:, 0:3]), inc=v)
            bc_ready = v.n
        else:
            em.wait("tensor", ld_rows, rows_ld)
            for j in range(4):
                if j > 0:
                    em.wait("tensor", v)
                em.op("tensor", lambda e, j_=j: e.matmul(
                    ps_bc[:, 0:512], ones_f[0:1, 0:128], rows[0:1, 512 * j_:512 * (j_ + 1)],
                    start=True, stop=True), inc=pe)
                em.wait("vector", pe)
                em.op("vector", lambda e, j_=j: e.tensor_copy(skb[:, 512 * j_:512 * (j_ + 1)], ps_bc[:, 0:512]), inc=v)
            em.op("vector", lambda e: e.tensor_reduce(srow[0:1, 0:1], rows[0:1, E:2 * E], AX.X,
                                                      ALU.max, apply_absolute_value=True))
            emit_int8_cols(em, srow[0:1, 0:1], srow[0:1, 1:2], srow[0:1, 2:3],
                           srow[0:1, 3:4], srow[0:1, 4:5])
            em.op("vector", lambda e: e.tensor_scalar(rows[0:1, 2 * E:3 * E], rows[0:1, E:2 * E],
                                                      srow[0:1, 1:2], None, ALU.mult))
            emit_round(em, "vector", rows[0:1, 2 * E:3 * E], rows[0:1, 2 * E:3 * E],
                       rowi[0:1, :], rows[0:1, 3 * E:4 * E], rows[0:1, 4 * E:5 * E])
            em.op("vector", lambda e: e.tensor_scalar(rows[0:1, 2 * E:3 * E], rows[0:1, 2 * E:3 * E],
                                                      127.0, -127.0, ALU.min, ALU.max))
            em.op("vector", lambda e: e.tensor_scalar(rows[0:1, 2 * E:3 * E], rows[0:1, 2 * E:3 * E],
                                                      srow[0:1, 2:3], None, ALU.mult), inc=v)
            for j in range(4):
                em.wait("tensor", v)
                em.op("tensor", lambda e, j_=j: e.matmul(
                    ps_bc[:, 0:512], ones_f[0:1, 0:128], rows[0:1, 2 * E + 512 * j_:2 * E + 512 * (j_ + 1)],
                    start=True, stop=True), inc=pe)
                em.wait("vector", pe)
                em.op("vector", lambda e, j_=j: e.tensor_copy(bqb[:, 512 * j_:512 * (j_ + 1)], ps_bc[:, 0:512]), inc=v)
            bc_ready = v.n
            ffnb_ready = scs.n

        for i in range(8):
            em.wait("sync", v, v.n)
            em.dma("sync", xnf[:, :], D["x_own"][128 * i:128 * (i + 1), :], ld_xnf)
            em.wait("vector", ld_xnf)
            em.wait("vector", v, bc_ready)
            em.wait("vector", pe, pe.n)
            em.op("vector", lambda e, i_=i: e.tensor_scalar(
                xnf[:, :], xnf[:, :],
                stats[:, 2 * i_:2 * i_ + 1], stats[:, 2 * i_ + 1:2 * i_ + 2],
                ALU.add, ALU.mult))
            if layer == 0:
                em.op("vector", lambda e: e.tensor_reduce(
                    bamax[:, :].rearrange("p (b k) -> p b k", k=16),
                    xnf[:, :].rearrange("p (b k) -> p b k", k=16),
                    AX.X, ALU.max, apply_absolute_value=True))
                emit_e4m3(em, bamax[:, :], cols2[:, 1:2], cols2[:, 2:3], bsq, invd, e4scr, sv, ss,
                          cst[:, 0:1], cst[:, 1:2])
                emit_e2m1(em, xnf[:, :], invd, bsq, xq[:, :], qscr, sv, ss, inc=v)
            else:
                em.op("vector", lambda e: e.tensor_reduce(
                    cols[:, 0:1], xnf[:, :], AX.X, ALU.max, apply_absolute_value=True))
                emit_int8_cols(em, cols[:, 0:1], cols[:, 1:2], stats[:, 16 + i:17 + i],
                               cols[:, 2:3], cols[:, 3:4])
                em.op("vector", lambda e: e.tensor_scalar(
                    ysb[:, :], xnf[:, :], cols[:, 1:2], None, ALU.mult))
                emit_round(em, "vector", ysb[:, :], xq[:, :], i1t[:, :],
                           gsb[:, :], f2b[:, :])
                em.op("vector", lambda e: e.tensor_scalar(
                    xq[:, :], xq[:, :], 127.0, -127.0, ALU.min, ALU.max), inc=v)
            em.wait("tensor", v)
            for q4 in range(4):
                pst = ps_tr[q4 % 2]
                em.wait("tensor", evt, evt.n - 1)
                for j4 in range(4):
                    j = 4 * q4 + j4
                    em.op("tensor", lambda e, jj=j, p=pst, o=j4: e.transpose(
                        p[:, 128 * o:128 * (o + 1)], xq[:, 128 * jj:128 * (jj + 1)], id_sb[:, :]),
                        inc=(pe if j4 == 3 else None))
                em.wait("scalar", pe)
                em.op("scalar", lambda e, q=q4, p=pst: e.copy(
                    xqT[:, 512 * q:512 * (q + 1)], p[:, 0:512]), inc=evt)
            em.wait("tensor", evt)
            em.wait("tensor", ld_wf, wf_mark)
            if layer == 0:
                em.wait("tensor", scs, ffnb_ready)
            for nb in range(4):
                bias = None
                if layer == 0:
                    bias = (ones_bf[0:1, 0:128], ffnb_bf[0:1, 512 * nb:512 * (nb + 1)])
                if layer == 0:
                    evac = (lambda p2, c=512 * nb: em.op("scalar", lambda e, p=p2, c_=c: e.activation(
                        gsb[:, c_:c_ + 512], ps_mm[p][:, 0:512], AF.Silu,
                        scale=cols2[:, 0:1]), inc=evq))
                else:
                    evac = (lambda p2, c=512 * nb, i_=i: em.op("scalar", lambda e, p=p2, c_=c, ii=i_: e.activation(
                        ysb[:, c_:c_ + 512], ps_mm[p][:, 0:512], AF.Copy,
                        scale=stats[:, 16 + ii:17 + ii]), inc=evq))
                mm_group(
                    lambda k: xqT[:, 128 * k:128 * (k + 1)],
                    lambda k, n=nb: wF[:, 2048 * k + 512 * n:2048 * k + 512 * (n + 1)],
                    16, bias, evac)
            if layer == 0:
                em.wait("sync", evq)
                em.dma("sync", D["g"][128 * i:128 * (i + 1), :], gsb[:, :], st_g)
                em.wait("scalar", st_g)
                em.wait("vector", st_g)
            else:
                em.wait("vector", evq)
                em.op("vector", lambda e: e.tensor_tensor(ysb[:, :], ysb[:, :], skb[:, :], ALU.mult))
                em.op("vector", lambda e: e.tensor_tensor(ysb[:, :], ysb[:, :], bqb[:, :], ALU.add))
                em.op("vector", lambda e: e.tensor_single_scalar(ysb[:, :], ysb[:, :], 0.0, ALU.max))
                em.op("vector", lambda e: e.tensor_reduce(cols[:, 0:1], ysb[:, :], AX.X,
                                                          ALU.max, apply_absolute_value=True))
                emit_int8_cols(em, cols[:, 0:1], cols[:, 1:2], cols[:, 4:5],
                               cols[:, 2:3], cols[:, 3:4])
                em.op("vector", lambda e: e.tensor_scalar(gsb[:, :], ysb[:, :], cols[:, 1:2], None, ALU.mult))
                emit_round(em, "vector", gsb[:, :], xq[:, :], i1t[:, :],
                           ysb[:, :], f2b[:, :])
                em.op("vector", lambda e: e.tensor_scalar(xq[:, :], xq[:, :], 127.0, -127.0, ALU.min, ALU.max))
                em.op("vector", lambda e: e.tensor_scalar(ysb[:, :], xq[:, :], cols[:, 4:5], None, ALU.mult), inc=v)
                em.wait("scalar", v)
                em.op("scalar", lambda e: e.activation(gsb[:, :], ysb[:, :], AF.Silu), inc=scs)
                em.wait("sync", scs)
                em.dma("sync", D["g"][128 * i:128 * (i + 1), :], gsb[:, :], st_g)
                em.wait("scalar", st_g)
                em.wait("vector", st_g)

        em.wait("gpsimd", st_s[0])
        em.wait("gpsimd", st_s[1])
        em.wait("gpsimd", st_g)
        em.finalize()
    return nc



def build_B(layer, dbg=False):
    nc = bass.Bass("TRN2", target_bir_lowering=False)
    D = {}
    def din(n, sh, dt=F32):
        D[n] = nc.dram_tensor(n, list(sh), dt, kind="ExternalInput").ap()
    def dout(n, sh, dt=F32):
        D[n] = nc.dram_tensor(n, list(sh), dt, kind="ExternalOutput").ap()

    nh = 16
    ngr = 4 if layer == 0 else 16
    kvw = 128 * ngr
    vew = 129 * ngr
    din("x_own", (1024, E))
    din("g", (1024, E))
    din("qfm", (E, 1024), BF16)
    din("kslot", (kvw, NSLOT_TILES * 128), BF16)
    din("vslot", (NSLOT_TILES * 128, vew), BF16)
    din("maskT", (128, 128), BF16)
    if layer == 1:
        din("woT", (E, E), BF16)
        din("ident", (128, 128), BF16)
    dout("xo", (1024, E))
    if dbg:
        dout("d_l", (1024, 16))
        dout("d_s", (1024, E))

    with ExitStack() as es:
        em = Em(nc, es)
        sb = lambda n, sh, dt=F32: es.enter_context(nc.sbuf_tensor(n, sh, dt))

        qsb = sb("qsb", [128, 16 * 1024], BF16)
        ksb = [sb(f"ksb{i}", [128, kvw], BF16) for i in range(2)]
        vsb = [sb(f"vsb{i}", [128, vew], BF16) for i in range(2)]
        ptb = sb("ptb", [128, 16 * 128], BF16)
        msk = sb("msk", [128, 128], BF16)
        lsb = sb("lsb", [128, 16])
        cst = sb("cst", [128, 1])
        lrec = sb("lrec", [128, 16])
        ssb = sb("ssb", [128, E])
        xg = [sb(f"xg{i}", [128, E]) for i in range(2)]
        ost = [sb(f"ost{i}", [128, E]) for i in range(2)]
        if layer == 1:
            s_all = sb("s_all", [128, 8 * E], BF16)
            wo_sb = sb("wo_sb", [128, 16 * 2048], BF16)
            sT = sb("sT", [128, 16 * 128], BF16)
            id_sb = sb("id_sb", [128, 128], BF16)

        ldk = [em.sem("ldk0"), em.sem("ldk1")]
        ld2 = em.sem("ld2")
        wol = em.sem("wol")
        pe_sc = em.sem("pe_sc")
        exd = em.sem("exd")
        mkd = em.sem("mkd")
        av = em.sem("av")
        epv = em.sem("epv")
        eps = em.sem("eps")
        v2 = em.sem("v2")
        st = em.sem("st")
        st_o = [em.sem("st_o0"), em.sem("st_o1")]
        pe = em.sem("pe")
        evq = em.sem("evq")
        evt = em.sem("evt")

        em.dma("sync", msk[:, :], D["maskT"][:, :], ld2)
        vini = em.sem("vini")
        em.op("vector", lambda e: e.memset(cst[:, 0:1], 0.1), inc=vini)
        em.wait("scalar", vini)
        for oc in range(16):
            em.dma("sync", qsb[:, 1024 * oc:1024 * (oc + 1)], D["qfm"][128 * oc:128 * (oc + 1), :], ld2)
        if layer == 1:
            em.dma("sync", id_sb[:, :], D["ident"][:, :], ld2)
            for ec in range(16):
                em.dma("gpsimd", wo_sb[:, 2048 * ec:2048 * (ec + 1)],
                       D["woT"][128 * ec:128 * (ec + 1), :], wol)
        q_ready = ld2.n

        with ExitStack() as ps_es:
            o_ps = [ps_es.enter_context(nc.psum_tensor(f"o_ps{t}", [128, 512], F32))
                    for t in range(6)]
            ps_sc = [ps_es.enter_context(nc.psum_tensor(f"ps_sc{i}", [128, 512], F32))
                     for i in range(2)]

            def o_slice(h, w=129):
                return o_ps[h // 3][:, (h % 3) * 129:(h % 3) * 129 + w]

            kv_free = [0, 0]
            av_prev = [0]
            for j, (koff, L) in enumerate(slot_offsets()):
                sq = j // 2
                qcol = 256 * sq + (128 if j % 2 == 1 else 0)
                for kt in range(L):
                    par = kt % 2
                    em.wait("sync", av, kv_free[par])
                    abs_off = (koff + kt) * 128
                    em.dma("sync", ksb[par][:, :].rearrange("p (g k) -> p g k", k=128),
                           D["kslot"][:, abs_off:abs_off + 128].rearrange("(g p) k -> p g k", p=128), ldk[par])
                    em.dma("sync", vsb[par][:, :], D["vslot"][abs_off:abs_off + 128, :], ldk[par])
                    em.wait("tensor", ldk[par])
                    if j == 0 and kt == 0:
                        em.wait("tensor", ld2, q_ready)
                    em.wait("scalar", av, av_prev[0])
                    for h in range(nh):
                        gc = h // 4 if layer == 0 else h
                        scp = h % 2
                        em.wait("tensor", exd, exd.n - 1)
                        if kt == 0 and h == 0:
                            em.wait("tensor", eps, eps.n)
                        em.op("tensor", lambda e, g_=gc, h_=h, p=par, sp=scp, q=qcol: e.matmul(
                            ps_sc[sp][:, 0:128],
                            ksb[p][:, 128 * g_:128 * (g_ + 1)],
                            qsb[:, 1024 * h_ + q:1024 * h_ + q + 128],
                            start=True, stop=True), inc=pe_sc)
                        em.wait("scalar", pe_sc)
                        if layer == 0:
                            em.op("scalar", lambda e, h_=h, sp=scp: e.activation(
                                ptb[:, 128 * h_:128 * (h_ + 1)], ps_sc[sp][:, 0:128],
                                AF.Exp, scale=RSQ), inc=exd)
                        else:
                            em.op("scalar", lambda e, h_=h, sp=scp: e.activation(
                                ptb[:, 128 * h_:128 * (h_ + 1)], ps_sc[sp][:, 0:128],
                                AF.Relu, bias=cst[:, 0:1], scale=RSQ), inc=exd)
                        if kt == 0:
                            if j == 0 and h == 0:
                                em.wait("vector", ld2, q_ready)
                            em.wait("vector", exd)
                            em.op("vector", lambda e, h_=h: e.tensor_tensor(
                                ptb[:, 128 * h_:128 * (h_ + 1)], ptb[:, 128 * h_:128 * (h_ + 1)],
                                msk[:, :], ALU.mult), inc=mkd)
                            em.wait("tensor", mkd)
                        else:
                            em.wait("tensor", exd)
                        em.op("tensor", lambda e, h_=h, g_=gc, p=par, k=kt, l=L: e.matmul(
                            o_slice(h_),
                            ptb[:, 128 * h_:128 * (h_ + 1)],
                            vsb[p][:, 129 * g_:129 * (g_ + 1)],
                            start=(k == 0 and h_ % 3 == 0),
                            stop=(k == l - 1 and (h_ % 3 == 2 or h_ == 15))),
                            inc=(av if h == nh - 1 else None))
                    kv_free[par] = av.n
                    av_prev[0] = av.n
                em.wait("vector", av)
                for t6 in range(6):
                    nhh = 3 if t6 < 5 else 1
                    em.op("vector", lambda e, t=t6, n=nhh: e.tensor_copy(
                        lsb[:, 3 * t:3 * t + n],
                        o_ps[t][:, 0:387].rearrange("p (hh o) -> p hh o", o=129)[:, 0:n, 128:129]))
                if layer == 1:
                    em.op("vector", lambda e: e.tensor_single_scalar(lsb[:, :], lsb[:, :], 1e-6, ALU.add))
                em.op("vector", lambda e: e.reciprocal(lrec[:, :], lsb[:, :]), inc=epv)
                em.wait("scalar", epv)
                if layer == 0:
                    em.wait("scalar", v2, v2.n)
                for h in range(nh):
                    if layer == 0:
                        em.op("scalar", lambda e, h_=h: e.activation(
                            ssb[:, 128 * h_:128 * (h_ + 1)], o_slice(h_, 128), AF.Copy,
                            scale=lrec[:, h_:h_ + 1]), inc=eps)
                    else:
                        em.op("scalar", lambda e, h_=h, j_=j: e.activation(
                            s_all[:, E * j_ + 128 * h_:E * j_ + 128 * (h_ + 1)], o_slice(h_, 128),
                            AF.Copy, scale=lrec[:, h_:h_ + 1]), inc=eps)
                if dbg and layer == 0:
                    em.wait("sync", eps)
                    em.dma("sync", D["d_l"][128 * j:128 * (j + 1), :], lsb[:, :], st)
                    em.dma("sync", D["d_s"][128 * j:128 * (j + 1), :], ssb[:, :], st)
                    em.wait("vector", st)
                    em.wait("scalar", st)
                if layer == 0:
                    em.wait("sync", v2, v2.n)
                    em.dma("sync", xg[0][:, :], D["x_own"][128 * j:128 * (j + 1), :], ld2)
                    em.dma("sync", xg[1][:, :], D["g"][128 * j:128 * (j + 1), :], ld2)
                    em.wait("vector", ld2)
                    em.wait("vector", eps)
                    opar = j % 2
                    em.wait("vector", st_o[opar], 16 * (j // 2))
                    em.op("vector", lambda e, o=opar: e.tensor_tensor(ost[o][:, :], ssb[:, :], xg[1][:, :], ALU.mult))
                    em.op("vector", lambda e, o=opar: e.tensor_tensor(ost[o][:, :], ost[o][:, :], xg[0][:, :], ALU.add), inc=v2)
                    em.wait("sync", v2)
                    em.dma("sync", D["xo"][128 * j:128 * (j + 1), :], ost[opar][:, :], st_o[opar])

        if layer == 1:
            ps_tr = [es.enter_context(nc.psum_tensor(f"ps_tr{i}", [128, 1024], BF16)) for i in range(2)]
            ps_mm = [es.enter_context(nc.psum_tensor(f"ps_mm{i}", [128, 512], F32)) for i in range(2)]
            em.wait("tensor", eps)
            grp = [0]
            ev_free = [0, 0]
            for i in range(8):
                for q4 in range(4):
                    pst = ps_tr[q4 % 2]
                    em.wait("tensor", evt, evt.n - 1)
                    for j4 in range(4):
                        jj = 4 * q4 + j4
                        em.op("tensor", lambda e, i_=i, c=jj, p=pst, o=j4: e.transpose(
                            p[:, 128 * o:128 * (o + 1)],
                            s_all[:, E * i_ + 128 * c:E * i_ + 128 * (c + 1)], id_sb[:, :]),
                            inc=(pe if j4 == 3 else None))
                    em.wait("scalar", pe)
                    em.op("scalar", lambda e, q=q4, p=pst: e.copy(
                        sT[:, 512 * q:512 * (q + 1)], p[:, 0:512]), inc=evt)
                em.wait("tensor", evt)
                if i == 0:
                    em.wait("tensor", wol)
                em.wait("sync", v2, v2.n)
                em.dma("sync", xg[0][:, :], D["x_own"][128 * i:128 * (i + 1), :], ld2)
                em.dma("sync", xg[1][:, :], D["g"][128 * i:128 * (i + 1), :], ld2)
                ldxg = ld2.n
                em.wait("scalar", v2, v2.n)
                for nb in range(4):
                    p2 = grp[0] % 2
                    grp[0] += 1
                    em.wait("tensor", evq, ev_free[p2])
                    for cc in range(16):
                        em.op("tensor", lambda e, c=cc, n=nb, p=p2: e.matmul(
                            ps_mm[p][:, 0:512],
                            sT[:, 128 * c:128 * (c + 1)],
                            wo_sb[:, 2048 * c + 512 * n:2048 * c + 512 * (n + 1)],
                            start=(c == 0), stop=(c == 15)), inc=(pe if cc == 15 else None))
                    em.wait("scalar", pe)
                    em.op("scalar", lambda e, p=p2, c=512 * nb: e.copy(
                        ssb[:, c:c + 512], ps_mm[p][:, 0:512]), inc=evq)
                    ev_free[p2] = evq.n
                em.wait("vector", evq)
                em.wait("vector", ld2, ldxg)
                opar = i % 2
                em.wait("vector", st_o[opar], 16 * (i // 2))
                em.op("vector", lambda e, o=opar: e.tensor_tensor(ost[o][:, :], ssb[:, :], xg[1][:, :], ALU.mult))
                em.op("vector", lambda e, o=opar: e.tensor_tensor(ost[o][:, :], ost[o][:, :], xg[0][:, :], ALU.add), inc=v2)
                em.wait("sync", v2)
                em.dma("sync", D["xo"][128 * i:128 * (i + 1), :], ost[opar][:, :], st_o[opar])

        em.wait("gpsimd", st_o[0])
        em.wait("gpsimd", st_o[1])
        em.wait("gpsimd", st)
        em.finalize()
    return nc



_PROGS = {}


def _get_prog(key, builder, *args):
    if key not in _PROGS:
        _PROGS[key] = builder(*args)
    return _PROGS[key]


def _bf(a):
    return np.ascontiguousarray(a.astype(BF) if a.dtype != BF else a)


def _build_slots(Kseq, Vseq, c, ngr):
    kvw = 128 * ngr
    vew = 129 * ngr
    ks = np.zeros((kvw, NSLOT_TILES * 128), dtype=BF)
    vs = np.zeros((NSLOT_TILES * 128, vew), dtype=BF)
    offs = slot_offsets()
    for j, (s, w) in enumerate(own_tiles(c)):
        koff, L = offs[j]
        order = [w] + list(range(w))
        for kk, tk in enumerate(order):
            a = (koff + kk) * 128
            ks[:, a:a + 128] = Kseq[s][:, tk * 128:(tk + 1) * 128]
            for g in range(ngr):
                vs[a:a + 128, 129 * g:129 * g + 128] = Vseq[s][tk * 128:(tk + 1) * 128,
                                                               128 * g:128 * (g + 1)]
                vs[a:a + 128, 129 * g + 128] = 1.0
    return ks, vs


def kernel(L0_attn_x, L0_attn_Wq, L0_attn_bq, L0_attn_Wk, L0_attn_bk, L0_attn_Wv,
           L0_attn_bv, L0_ffn_weight, L0_ffn_bias, L0_ffn_x_tensor_scale,
           L1_attn_Wq, L1_attn_Wk, L1_attn_Wv, L1_attn_Wo, L1_ffn_kernel, L1_ffn_bias):
    x = np.ascontiguousarray(np.asarray(L0_attn_x, np.float32))
    ident = np.eye(128, dtype=BF)
    ident_f = np.eye(128, dtype=np.float32)
    maskT = np.triu(np.ones((128, 128), dtype=np.float32)).astype(BF)
    cores = list(range(8))

    w0 = np.ascontiguousarray(np.asarray(L0_ffn_weight, np.float32))
    k1 = np.ascontiguousarray(np.asarray(L1_ffn_kernel, np.float32))
    wq1 = np.ascontiguousarray(np.asarray(L1_attn_Wq, np.float32))
    wk1 = np.ascontiguousarray(np.asarray(L1_attn_Wk, np.float32))
    wv1 = np.ascontiguousarray(np.asarray(L1_attn_Wv, np.float32))
    wo1 = np.ascontiguousarray(np.asarray(L1_attn_Wo, np.float32))

    nc = _get_prog("prep", build_prep)
    im = []
    for c in cores:
        sl = slice(256 * c, 256 * (c + 1))
        im.append({"w0_full": w0, "w0_own": w0[sl], "k1_own": np.ascontiguousarray(k1[:, sl]),
                   "wq1_own": wq1[sl], "wk1_own": wk1[sl], "wv1_own": wv1[sl],
                   "wo1_own": wo1[sl], "ident": ident, "ident_f": ident_f})
    res = run_bass_kernel_spmd(nc, im, core_ids=cores).results
    wdqT = np.concatenate([r["wdqT_s"] for r in res], axis=1)
    kq = np.concatenate([r["kq_s"] for r in res], axis=1)
    kscale_row = np.concatenate([r["kscale_s"][:, 0] for r in res]).reshape(1, E).astype(np.float32)
    wqT = np.concatenate([r["wqT_s"] for r in res], axis=1)
    wkT = np.concatenate([r["wkT_s"] for r in res], axis=1)
    wvT = np.concatenate([r["wvT_s"] for r in res], axis=1)
    woT = np.concatenate([r["woT_s"] for r in res], axis=1)
    wts = np.asarray(res[0]["wts"], np.float32).reshape(1, 1)

    def gather_x(xf):
        return [np.ascontiguousarray(np.concatenate(
            [xf[s, 128 * w:128 * (w + 1)] for (s, w) in own_tiles(c)], axis=0)) for c in cores]

    def run_layer(xf, layer):
        xowns = gather_x(xf)
        ncA = _get_prog(("A", layer), build_A, layer)
        im = []
        for c in cores:
            d = {"x_own": xowns[c], "ident": ident}
            if layer == 0:
                d.update({"wq": np.ascontiguousarray(np.asarray(L0_attn_Wq, np.float32)),
                          "wk": np.ascontiguousarray(np.asarray(L0_attn_Wk, np.float32)),
                          "wv": np.ascontiguousarray(np.asarray(L0_attn_Wv, np.float32)),
                          "bq_row": np.asarray(L0_attn_bq, np.float32).reshape(1, E),
                          "bk_row": np.asarray(L0_attn_bk, np.float32).reshape(1, 512),
                          "bv_row": np.asarray(L0_attn_bv, np.float32).reshape(1, 512),
                          "wdqT": wdqT, "ffn_bias": np.asarray(L0_ffn_bias, np.float32).reshape(1, E),
                          "wts": wts,
                          "xts": np.asarray(L0_ffn_x_tensor_scale, np.float32).reshape(1, 1)})
            else:
                d.update({"wqT": wqT, "wkT": wkT, "wvT": wvT, "kq": kq,
                          "kscale_row": kscale_row,
                          "ffn_bias": np.asarray(L1_ffn_bias, np.float32).reshape(1, E)})
            im.append(d)
        resA = run_bass_kernel_spmd(ncA, im, core_ids=cores).results

        ngr = 4 if layer == 0 else 16
        kvw = 128 * ngr
        Kseq = [np.zeros((kvw, S), dtype=BF) for _ in range(B)]
        Vseq = [np.zeros((S, kvw), dtype=BF) for _ in range(B)]
        for c in cores:
            for j, (s, w) in enumerate(own_tiles(c)):
                Kseq[s][:, 128 * w:128 * (w + 1)] = resA[c]["kfm"][:, 128 * j:128 * (j + 1)]
                Vseq[s][128 * w:128 * (w + 1), :] = resA[c]["vtm"][128 * j:128 * (j + 1), :]

        ncB = _get_prog(("B", layer), build_B, layer)
        im = []
        for c in cores:
            ks, vs = _build_slots(Kseq, Vseq, c, ngr)
            d = {"x_own": xowns[c], "g": resA[c]["g"], "qfm": resA[c]["qfm"],
                 "kslot": ks, "vslot": vs, "maskT": maskT}
            if layer == 1:
                d.update({"woT": woT, "ident": ident})
            im.append(d)
        resB = run_bass_kernel_spmd(ncB, im, core_ids=cores).results

        xout = np.empty((B, S, E), np.float32)
        for c in cores:
            for j, (s, w) in enumerate(own_tiles(c)):
                xout[s, 128 * w:128 * (w + 1)] = resB[c]["xo"][128 * j:128 * (j + 1)]
        return xout

    x1 = run_layer(x, 0)
    x2 = run_layer(x1, 1)
    return x2
